# version 11
# baseline (speedup 1.0000x reference)
import numpy as np
import ml_dtypes
from contextlib import ExitStack

import concourse.bass as bass
import concourse.mybir as mybir
from concourse.bass_utils import run_bass_kernel_spmd

F32 = mybir.dt.float32
BF16 = mybir.dt.bfloat16
AF = mybir.ActivationFunctionType
OP = mybir.AluOpType
AX = mybir.AxisListType.X

E = 2048
S = 4096
BATCH = 4
H = 16
NE = 16
HD = 1408
NHC = HD // 128
HS = 2816
NSC = HS // 128
INV_SQRT_D = float(1.0 / np.sqrt(128.0))
ALIBI_SHIFT = -20.0
NEG = -1e30
BF = ml_dtypes.bfloat16
DBG_SPH = (0, 0)


def _alibi_slopes(n):
    import math
    def pow2_slopes(n):
        start = 2 ** (-2 ** (-(math.log2(n) - 3)))
        return [start * start ** i for i in range(n)]
    if math.log2(n).is_integer():
        return pow2_slopes(n)
    c = 2 ** math.floor(math.log2(n))
    return pow2_slopes(c) + _alibi_slopes(2 * c)[::2][:n - c]


class Em:

    ENGINES = ("sync", "gpsimd", "vector", "scalar", "tensor")

    def __init__(self, nc):
        self.nc = nc
        self.ops = {e: [] for e in self.ENGINES}
        self.sems = {}
        self.counts = {}

    def sem(self, name, ctx):
        s = ctx.enter_context(self.nc.semaphore(name))
        self.sems[name] = s
        self.counts[name] = 0

    def _push(self, eng, waits, fn, incs):
        sems = self.sems
        waits = [(w, v) for (w, v) in waits if v > 0]
        def run(engine, _w=tuple(waits), _f=fn, _i=tuple(incs)):
            for (w, v) in _w:
                engine.wait_ge(sems[w], v)
            ins = _f(engine)
            for (s, n) in _i:
                ins = ins.then_inc(sems[s], n)

        self.ops[eng].append(run)

    def op(self, eng, fn, wait=(), inc=()):
        incs = []
        for s in inc:
            self.counts[s] += 1
            incs.append((s, 1))
        self._push(eng, list(wait), fn, incs)
        return self.counts[inc[0]] if inc else None

    def dma(self, eng, fn, sem, wait=(), serialize=False):
        waits = list(wait)
        if serialize:
            waits.append((sem, self.counts[sem]))
        self.counts[sem] += 16
        v = self.counts[sem]
        self._push(eng, waits, fn, [(sem, 16)])
        return v

    def replay(self, block):
        ops = self.ops
        @block.sync
        def _(e):
            for f in ops["sync"]: f(e)
        @block.gpsimd
        def _(e):
            for f in ops["gpsimd"]: f(e)
        @block.vector
        def _(e):
            for f in ops["vector"]: f(e)
        @block.scalar
        def _(e):
            for f in ops["scalar"]: f(e)
        @block.tensor
        def _(e):
            for f in ops["tensor"]: f(e)



def host_prep(inp):
    x = np.asarray(inp["L0_attn_x"])
    Wq = np.asarray(inp["L0_attn_Wq"]); bq = np.asarray(inp["L0_attn_bq"])
    Wk = np.asarray(inp["L0_attn_Wk"]); bk = np.asarray(inp["L0_attn_bk"])
    Wv = np.asarray(inp["L0_attn_Wv"]); bv = np.asarray(inp["L0_attn_bv"])
    Wo = np.asarray(inp["L0_attn_Wo"]); bo = np.asarray(inp["L0_attn_bo"])
    gw = np.asarray(inp["L0_ffn_gate_w"]); gb = np.asarray(inp["L0_ffn_gate_b"])
    w1 = np.asarray(inp["L0_ffn_w1"]); b1 = np.asarray(inp["L0_ffn_b1"])
    w2 = np.asarray(inp["L0_ffn_w2"]); b2 = np.asarray(inp["L0_ffn_b2"])
    w3 = np.asarray(inp["L0_ffn_w3"]); b3 = np.asarray(inp["L0_ffn_b3"])
    ws1 = np.asarray(inp["L0_ffn_ws1"]); bs1 = np.asarray(inp["L0_ffn_bs1"])
    ws2 = np.asarray(inp["L0_ffn_ws2"]); bs2 = np.asarray(inp["L0_ffn_bs2"])
    ws3 = np.asarray(inp["L0_ffn_ws3"]); bs3 = np.asarray(inp["L0_ffn_bs3"])

    C = np.ascontiguousarray
    sh = {}
    def sqW(W):
        return C(W.reshape(16, 128, 16, 128).transpose(0, 3, 2, 1))
    sh["WqH"] = sqW(Wq); sh["WkH"] = sqW(Wk)
    sh["WvH"] = C(Wv.T.reshape(16, 128, 2048))
    sh["WoH"] = C(Wo.reshape(16, 128, 16, 128).transpose(2, 3, 0, 1))
    sh["bqH"] = C(bq.reshape(16, 128).T)
    sh["bkH"] = C(bk.reshape(16, 128).T)
    sh["boH"] = C(bo.reshape(16, 128).T)
    sh["bvH"] = C(bv.reshape(1, 2048))
    sh["w1H"] = C(w1.reshape(16, NHC, 128, 16, 128).transpose(0, 1, 4, 3, 2))
    sh["w3H"] = C(w3.reshape(16, NHC, 128, 16, 128).transpose(0, 1, 4, 3, 2))
    sh["w2H"] = C(w2.reshape(16, 16, 128, NHC, 128).transpose(0, 1, 4, 3, 2))
    sh["ws1H"] = C(ws1.reshape(NSC, 128, 16, 128).transpose(0, 3, 2, 1))
    sh["ws3H"] = C(ws3.reshape(NSC, 128, 16, 128).transpose(0, 3, 2, 1))
    sh["ws2H"] = C(ws2.reshape(16, 128, NSC, 128).transpose(0, 3, 2, 1))
    sh["b1H"] = C(b1.reshape(16 * NHC, 128).T)
    sh["b3H"] = C(b3.reshape(16 * NHC, 128).T)
    sh["b2H"] = C(b2.reshape(16, 16 * 128))
    sh["bs1H"] = C(bs1.reshape(NSC, 128).T)
    sh["bs3H"] = C(bs3.reshape(NSC, 128).T)
    sh["bs2H"] = C(bs2.reshape(1, 2048))
    sh["gwH"] = C(gw.T.reshape(16, 128, 16))
    sh["gbH"] = C(np.broadcast_to(gb, (128, 16)))
    sh["identF"] = np.eye(128, dtype=np.float32)
    sh["identB"] = np.eye(128, dtype=BF)

    sh["ws2H"] = C(ws2.reshape(16, 128, NSC, 128).transpose(0, 3, 2, 1))

    slopes = np.asarray(_alibi_slopes(H), dtype=np.float64)
    xT = x.transpose(0, 2, 1)

    per_core = []
    for c in range(8):
        b, half = c // 2, c % 2
        m = {}
        mine = xT[b][:, half * 2048:(half + 1) * 2048]
        partner = xT[b][:, (1 - half) * 2048:(2 - half) * 2048]
        m["xTp"] = C(np.concatenate([partner, mine], axis=1))
        gpos = np.empty(4096, dtype=np.float64)
        gpos[:2048] = np.arange(2048) + (1 - half) * 2048
        gpos[2048:] = np.arange(2048) + half * 2048
        alb = np.full((4, 32, 128, 16), NEG, dtype=np.float64)
        for sp in range(4):
            q_end = half * 2048 + sp * 512 + 512
            for kc in range(32):
                j = gpos[kc * 128:(kc + 1) * 128]
                col = slopes[None, :] * j[:, None] + ALIBI_SHIFT
                alb[sp, kc] = np.where((j < q_end)[:, None], col, NEG)
        m["alb"] = C(alb.transpose(2, 0, 1, 3).reshape(128, 2048).astype(np.float32))
        mypos = np.arange(2048) + half * 2048
        m["qb"] = C((-slopes[:, None] * mypos[None, :] * np.sqrt(128.0)).astype(BF))
        mb = np.zeros((4, 4, 128, 512), dtype=np.float64)
        for sp in range(4):
            qg = half * 2048 + sp * 512 + np.arange(512)
            for j in range(4):
                kc = 16 + 4 * sp + j
                kg = gpos[kc * 128:(kc + 1) * 128]
                mb[sp, j] = np.where(kg[:, None] > qg[None, :], NEG, 0.0)
        m["maskb"] = C(mb.transpose(2, 0, 1, 3).reshape(128, 8192).astype(BF))
        per_core.append(m)
    return sh, per_core



def build_program(phases=4, dump=(), dbg4=False):
    nc = bass.Bass("TRN2", target_bir_lowering=False, debug=False)

    din = {}
    def DI(name, shape, dt=F32):
        din[name] = nc.dram_tensor(name, list(shape), dt, kind="ExternalInput").ap()
        return din[name]

    xTp = DI("xTp", (2048, 4096))
    WqH = DI("WqH", (16, 128, 16, 128)); WkH = DI("WkH", (16, 128, 16, 128))
    WvH = DI("WvH", (16, 128, 2048)); WoH = DI("WoH", (16, 128, 16, 128))
    bqH = DI("bqH", (128, 16)); bkH = DI("bkH", (128, 16)); boH = DI("boH", (128, 16))
    bvH = DI("bvH", (1, 2048))
    alb_d = DI("alb", (128, 2048))
    qb_d = DI("qb", (16, 2048), BF16)
    maskb_d = DI("maskb", (128, 8192), BF16)
    identF = DI("identF", (128, 128))
    identB = DI("identB", (128, 128), BF16)
    w1H = DI("w1H", (16, NHC, 128, 16, 128)); w3H = DI("w3H", (16, NHC, 128, 16, 128))
    w2H = DI("w2H", (16, 16, 128, NHC, 128))
    ws1H = DI("ws1H", (NSC, 128, 16, 128)); ws3H = DI("ws3H", (NSC, 128, 16, 128))
    ws2H = DI("ws2H", (16, 128, NSC, 128))
    b1H = DI("b1H", (128, 16 * NHC)); b3H = DI("b3H", (128, 16 * NHC))
    b2H = DI("b2H", (16, 2048))
    bs1H = DI("bs1H", (128, NSC)); bs3H = DI("bs3H", (128, NSC))
    bs2H = DI("bs2H", (1, 2048))
    gwH = DI("gwH", (16, 128, 16)); gbH = DI("gbH", (128, 16))

    outT = nc.dram_tensor("outT", [2048, 2048], F32, kind="ExternalOutput").ap()
    dbg_t = {}
    if dbg4:
        dbg_t["xfT"] = nc.dram_tensor("dbg_xfT", [128, 16384], BF16, kind="ExternalOutput").ap()
        dbg_t["y"] = nc.dram_tensor("dbg_y", [128, 16384], F32, kind="ExternalOutput").ap()
        dbg_t["cwT"] = nc.dram_tensor("dbg_cwT", [16, 1024], F32, kind="ExternalOutput").ap()
        dbg_t["p0"] = nc.dram_tensor("dbg_p0", [128, 32 * 512], BF16, kind="ExternalOutput").ap()
        dbg_t["qbc"] = nc.dram_tensor("dbg_qbc", [128, 512], BF16, kind="ExternalOutput").ap()
        dbg_t["g0"] = nc.dram_tensor("dbg_g0", [128, NHC * 1024], BF16, kind="ExternalOutput").ap()
        dbg_t["sc0"] = nc.dram_tensor("dbg_sc0", [128, 128], F32, kind="ExternalOutput").ap()
        dbg_t["y0"] = nc.dram_tensor("dbg_y0", [128, 16384], F32, kind="ExternalOutput").ap()
        dbg_t["cb0"] = nc.dram_tensor("dbg_cb0", [128, 1024], F32, kind="ExternalOutput").ap()

    kT_sc = nc.dram_tensor("kT_sc", [2048, 4096], BF16, kind="Internal").ap()
    v3_sc = nc.dram_tensor("v3_sc", [16, 128, 32, 128], BF16, kind="Internal").ap()
    qT_sc = nc.dram_tensor("qT_sc", [2048, 2048], BF16, kind="Internal").ap()
    x2T_sc = nc.dram_tensor("x2T_sc", [2048, 2048], F32, kind="Internal").ap()
    scratch = {"kT_sc": kT_sc, "v3_sc": v3_sc, "qT_sc": qT_sc, "x2T_sc": x2T_sc}
    dumps = {}
    for name in dump:
        t = scratch[name].tensor
        dumps[name] = nc.dram_tensor("dump_" + name, list(t.shape),
                                     t.dtype, kind="ExternalOutput").ap()

    root = ExitStack()
    em = Em(nc)
    CN = em.counts

    with root:
        sbr = lambda n, s, d: root.enter_context(nc.sbuf_tensor(n, list(s), d))
        for s in ("dS", "dG", "pe", "sc", "ve", "sy", "gp", "fin"):
            em.sem(s, root)

        ones_row = sbr("ones_row", (1, 128), BF16)
        ones_col = sbr("ones_col", (128, 1), BF16)
        ones_q = sbr("ones_q", (1, 512), BF16)
        o128 = sbr("o128", (128, 128), BF16)
        id_f = sbr("id_f", (128, 128), F32)
        id_b = sbr("id_b", (128, 128), BF16)
        bq_s = sbr("bq_s", (128, 16), F32)
        bk_s = sbr("bk_s", (128, 16), F32)
        bo_s = sbr("bo_s", (128, 16), F32)
        gb_s = sbr("gb_s", (128, 16), F32)
        gw_s = sbr("gw_s", (128, 256), F32)
        b1_s = sbr("b1_s", (128, 16 * NHC), F32)
        b3_s = sbr("b3_s", (128, 16 * NHC), F32)
        bs1_s = sbr("bs1_s", (128, NSC), F32)
        bs3_s = sbr("bs3_s", (128, NSC), F32)
        eps_s = sbr("eps_s", (1, 1), F32)
        m_row = sbr("m_row", (1, 512), F32)
        v_row = sbr("v_row", (1, 512), F32)
        t_row = sbr("t_row", (1, 512), F32)
        a_row = sbr("a_row", (1, 512), F32)
        b_row = sbr("b_row", (1, 512), F32)
        a_hi = sbr("a_hi", (1, 512), BF16); a_lo = sbr("a_lo", (1, 512), BF16)
        b_hi = sbr("b_hi", (1, 512), BF16); b_lo = sbr("b_lo", (1, 512), BF16)
        ab_f = sbr("ab_f", (128, 512), F32)
        bb_f = sbr("bb_f", (128, 512), F32)

        for nm, dst, src in (("idf", id_f, identF), ("idb", id_b, identB),
                             ("bq", bq_s, bqH), ("bk", bk_s, bkH), ("bo", bo_s, boH),
                             ("gb", gb_s, gbH),
                             ("b1", b1_s, b1H), ("b3", b3_s, b3H),
                             ("bs1", bs1_s, bs1H), ("bs3", bs3_s, bs3H)):
            t_consts = em.dma("sync", lambda e, d=dst, s=src: e.dma_start(out=d[:], in_=s), "dS")
        t_gw = em.dma("sync", lambda e: e.dma_start(
            out=gw_s[:].rearrange("p (a c) -> p a c", a=16), in_=gwH.transpose([1, 0, 2])), "dS")

        em.op("vector", lambda e: e.memset(ones_row[:], 1.0))
        em.op("vector", lambda e: e.memset(ones_col[:], 1.0))
        em.op("vector", lambda e: e.memset(ones_q[:], 1.0))
        em.op("vector", lambda e: e.memset(o128[:], 1.0 / 128.0))
        t_ones = em.op("vector", lambda e: e.memset(eps_s[:], 1e-5), inc=["ve"])

        def hilo_split(row_f, hi_b, lo_b, tmp_f, wait):
            ta = em.op("vector", lambda e: e.tensor_copy(out=hi_b[:], in_=row_f[:]),
                       wait=wait, inc=["ve"])
            tb = em.op("vector", lambda e: e.tensor_tensor(out=tmp_f[:], in0=row_f[:], in1=hi_b[:], op=OP.subtract),
                       wait=[("ve", ta)], inc=["ve"])
            return em.op("vector", lambda e: e.tensor_copy(out=lo_b[:], in_=tmp_f[:]),
                         wait=[("ve", tb)], inc=["ve"])

        def ln_evac(p_sum, p_sq, t_stats, nfeat, extra_wait=()):
            inv = 1.0 / nfeat
            em.op("scalar", lambda e: e.mul(out=m_row[:], in_=p_sum[:], mul=inv),
                  wait=[("pe", t_stats)] + list(extra_wait), inc=["sc"])
            return em.op("scalar", lambda e: e.mul(out=v_row[:], in_=p_sq[:], mul=inv),
                         wait=[("sc", CN["sc"])], inc=["sc"])

        def ln_rest(p_ab, p_bb, t_evac, extra_wait=()):
            t2 = em.op("vector", lambda e: e.tensor_mul(out=t_row[:], in0=m_row[:], in1=m_row[:]),
                       wait=[("sc", t_evac)] + list(extra_wait), inc=["ve"])
            t3 = em.op("vector", lambda e: e.tensor_sub(out=v_row[:], in0=v_row[:], in1=t_row[:]),
                       wait=[("ve", t2)], inc=["ve"])
            t4 = em.op("scalar", lambda e: e.activation(out=a_row[:], in_=v_row[:], func=AF.Ln,
                                                        bias=eps_s[0:1, 0:1]),
                       wait=[("ve", t3)], inc=["sc"])
            t5 = em.op("scalar", lambda e: e.activation(out=a_row[:], in_=a_row[:], func=AF.Exp, scale=-0.5),
                       wait=[("sc", t4)], inc=["sc"])
            t6 = em.op("vector", lambda e: e.tensor_mul(out=b_row[:], in0=m_row[:], in1=a_row[:]),
                       wait=[("sc", t5)], inc=["ve"])
            t7 = em.op("vector", lambda e: e.tensor_copy(out=a_hi[:], in_=a_row[:]),
                       wait=[("ve", t6)], inc=["ve"])
            t8 = em.op("vector", lambda e: e.tensor_sub(out=t_row[:], in0=a_row[:], in1=a_hi[:]),
                       wait=[("ve", t7)], inc=["ve"])
            t9 = em.op("vector", lambda e: e.tensor_copy(out=a_lo[:], in_=t_row[:]),
                       wait=[("ve", t8)], inc=["ve"])
            t10 = em.op("vector", lambda e: e.tensor_copy(out=b_hi[:], in_=b_row[:]),
                        wait=[("ve", t9)], inc=["ve"])
            t11 = em.op("vector", lambda e: e.tensor_sub(out=t_row[:], in0=b_row[:], in1=b_hi[:]),
                        wait=[("ve", t10)], inc=["ve"])
            t12 = em.op("vector", lambda e: e.tensor_copy(out=b_lo[:], in_=t_row[:]),
                        wait=[("ve", t11)], inc=["ve"])
            em.op("tensor", lambda e: e.matmul(out=p_ab[:], lhsT=ones_row[:], rhs=a_hi[:],
                                               start=True, stop=False), wait=[("ve", t9)])
            em.op("tensor", lambda e: e.matmul(out=p_ab[:], lhsT=ones_row[:], rhs=a_lo[:],
                                               start=False, stop=True))
            em.op("tensor", lambda e: e.matmul(out=p_bb[:], lhsT=ones_row[:], rhs=b_hi[:],
                                               start=True, stop=False), wait=[("ve", t12)])
            t_pe = em.op("tensor", lambda e: e.matmul(out=p_bb[:], lhsT=ones_row[:], rhs=b_lo[:],
                                                      start=False, stop=True), inc=["pe"])
            em.op("scalar", lambda e: e.copy(out=ab_f[:], in_=p_ab[:]),
                  wait=[("pe", t_pe)], inc=["sc"])
            t_ab = em.op("scalar", lambda e: e.copy(out=bb_f[:], in_=p_bb[:]),
                         wait=[("sc", CN["sc"])], inc=["sc"])
            return t_ab

        def ln_rows_and_bcast(p_sum, p_sq, p_ab, p_bb, t_stats, nfeat, extra_wait=()):
            inv = 1.0 / nfeat
            em.op("scalar", lambda e: e.mul(out=m_row[:], in_=p_sum[:], mul=inv),
                  wait=[("pe", t_stats)] + list(extra_wait), inc=["sc"])
            t1 = em.op("scalar", lambda e: e.mul(out=v_row[:], in_=p_sq[:], mul=inv),
                       wait=[("sc", CN["sc"])], inc=["sc"])
            t2 = em.op("vector", lambda e: e.tensor_mul(out=t_row[:], in0=m_row[:], in1=m_row[:]),
                       wait=[("sc", t1)], inc=["ve"])
            t3 = em.op("vector", lambda e: e.tensor_sub(out=v_row[:], in0=v_row[:], in1=t_row[:]),
                       wait=[("ve", t2)], inc=["ve"])
            t4 = em.op("scalar", lambda e: e.activation(out=a_row[:], in_=v_row[:], func=AF.Ln,
                                                        bias=eps_s[0:1, 0:1]),
                       wait=[("ve", t3)], inc=["sc"])
            t5 = em.op("scalar", lambda e: e.activation(out=a_row[:], in_=a_row[:], func=AF.Exp, scale=-0.5),
                       wait=[("sc", t4)], inc=["sc"])
            t6 = em.op("vector", lambda e: e.tensor_mul(out=b_row[:], in0=m_row[:], in1=a_row[:]),
                       wait=[("sc", t5)], inc=["ve"])
            t7 = em.op("vector", lambda e: e.tensor_copy(out=a_hi[:], in_=a_row[:]),
                       wait=[("ve", t6)], inc=["ve"])
            t8 = em.op("vector", lambda e: e.tensor_sub(out=t_row[:], in0=a_row[:], in1=a_hi[:]),
                       wait=[("ve", t7)], inc=["ve"])
            t9 = em.op("vector", lambda e: e.tensor_copy(out=a_lo[:], in_=t_row[:]),
                       wait=[("ve", t8)], inc=["ve"])
            t10 = em.op("vector", lambda e: e.tensor_copy(out=b_hi[:], in_=b_row[:]),
                        wait=[("ve", t9)], inc=["ve"])
            t11 = em.op("vector", lambda e: e.tensor_sub(out=t_row[:], in0=b_row[:], in1=b_hi[:]),
                        wait=[("ve", t10)], inc=["ve"])
            t12 = em.op("vector", lambda e: e.tensor_copy(out=b_lo[:], in_=t_row[:]),
                        wait=[("ve", t11)], inc=["ve"])
            em.op("tensor", lambda e: e.matmul(out=p_ab[:], lhsT=ones_row[:], rhs=a_hi[:],
                                               start=True, stop=False), wait=[("ve", t9)])
            em.op("tensor", lambda e: e.matmul(out=p_ab[:], lhsT=ones_row[:], rhs=a_lo[:],
                                               start=False, stop=True))
            em.op("tensor", lambda e: e.matmul(out=p_bb[:], lhsT=ones_row[:], rhs=b_hi[:],
                                               start=True, stop=False), wait=[("ve", t12)])
            t_pe = em.op("tensor", lambda e: e.matmul(out=p_bb[:], lhsT=ones_row[:], rhs=b_lo[:],
                                                      start=False, stop=True), inc=["pe"])
            em.op("scalar", lambda e: e.copy(out=ab_f[:], in_=p_ab[:]),
                  wait=[("pe", t_pe)], inc=["sc"])
            t_ab = em.op("scalar", lambda e: e.copy(out=bb_f[:], in_=p_bb[:]),
                         wait=[("sc", CN["sc"])], inc=["sc"])
            return t_ab

        def join():
            em.op("sync", lambda e: e.nop(), wait=[("dS", CN["dS"])], inc=["sy"])
            em.op("gpsimd", lambda e: e.nop(), wait=[("dG", CN["dG"])], inc=["gp"])
            em.op("vector", lambda e: e.nop(), inc=["ve"])
            em.op("scalar", lambda e: e.nop(), inc=["sc"])
            em.op("tensor", lambda e: e.nop(), inc=["pe"])
            tg = [(s, CN[s]) for s in ("sy", "gp", "ve", "sc", "pe")]
            for e in ("sync", "gpsimd", "vector", "scalar", "tensor"):
                em.op(e, lambda eng: eng.nop(), wait=tg)

        ph1 = ExitStack()
        with ph1:
            sb1 = lambda n, s, d: ph1.enter_context(nc.sbuf_tensor(n, list(s), d))
            ps1 = lambda n, s: ph1.enter_context(nc.psum_tensor(n, list(s), F32))
            x_f = sb1("x_f", (128, 8192), F32)
            xb = [sb1(f"xb{i}", (128, 8192), BF16) for i in range(4)]
            xsq = sb1("xsq", (128, 8192), BF16)
            wk_s = [sb1(f"wk_s{i}", (128, 2048), BF16) for i in range(2)]
            wq_s = [sb1(f"wq_s{i}", (128, 2048), BF16) for i in range(2)]
            wv_s = sb1("wv_s", (128, 16 * 1024), BF16)
            kq_ev = [sb1(f"kq_ev{i}", (128, 512), BF16) for i in range(4)]
            v_ev = [sb1(f"v_ev{i}", (128, 512), BF16) for i in range(4)]
            bv_row = sb1("bv_row", (1, 2048), F32)
            bv_hi = sb1("bv_hi", (1, 2048), BF16)
            bv_lo = sb1("bv_lo", (1, 2048), BF16)

            p_sum = ps1("p_sum", (1, 512))
            p_sq = ps1("p_sq", (1, 512))
            p_ab = ps1("p_ab", (128, 512))
            p_bb = ps1("p_bb", (128, 512))
            p_kq = [ps1(f"p_kq{i}", (128, 512)) for i in range(2)]
            p_v = [ps1(f"p_v{i}", (128, 512)) for i in range(2)]

            t_bvl = em.dma("sync", lambda e: e.dma_start(out=bv_row[:], in_=bvH), "dS",
                           serialize=True)
            t_bv = hilo_split(bv_row, bv_hi, bv_lo, bv_row, [("dS", t_bvl)])

            rel = {}
            kq_ev_tok = [0, 0, 0, 0]
            kq_ev_i = [0]
            v_ev_tok = [0, 0, 0, 0]
            v_ev_i = [0]

            def xln_span(gs, bi):
                w0 = [rel["x_f"]] if "x_f" in rel else []
                tx = 0
                for ec in range(16):
                    tx = em.dma("sync", lambda e, ec=ec, gs=gs: e.dma_start(
                        out=x_f[:, ec * 512:(ec + 1) * 512],
                        in_=xTp[ec * 128:(ec + 1) * 128, gs * 512:(gs + 1) * 512]),
                        "dS", wait=(w0 if ec == 0 else ()))
                wc = [("dS", tx)]
                if f"xb{bi}" in rel:
                    wc.append(rel[f"xb{bi}"])
                if "xsq" in rel:
                    wsq = [rel["xsq"]]
                else:
                    wsq = []
                t_cast = em.op("vector", lambda e, bi=bi: e.tensor_copy(out=xb[bi][:], in_=x_f[:]),
                               wait=wc, inc=["ve"])
                t_sq = em.op("scalar", lambda e: e.activation(out=xsq[:], in_=x_f[:], func=AF.Square),
                             wait=[("dS", tx)] + wsq, inc=["sc"])
                wst = [("ve", t_cast)]
                if "p_stats" in rel:
                    wst.append(rel["p_stats"])
                for ec in range(16):
                    em.op("tensor", lambda e, ec=ec, bi=bi: e.matmul(
                        out=p_sum[:], lhsT=ones_col[:], rhs=xb[bi][:, ec * 512:(ec + 1) * 512],
                        start=(ec == 0), stop=(ec == 15)), wait=(wst if ec == 0 else ()))
                t_stats = 0
                for ec in range(16):
                    t_stats = em.op("tensor", lambda e, ec=ec: e.matmul(
                        out=p_sq[:], lhsT=ones_col[:], rhs=xsq[:, ec * 512:(ec + 1) * 512],
                        start=(ec == 0), stop=(ec == 15)),
                        wait=([("sc", t_sq)] if ec == 0 else ()),
                        inc=(["pe"] if ec == 15 else ()))
                rel["xsq"] = ("pe", t_stats)
                ew = [rel["p_abb"]] if "p_abb" in rel else []
                t_ab = ln_rows_and_bcast(p_sum, p_sq, p_ab, p_bb, t_stats, 2048.0,
                                         extra_wait=ew + [("ve", t_ones)])
                rel["p_stats"] = ("sc", t_ab)
                tv = 0
                for ec in range(16):
                    tm = em.op("vector", lambda e, ec=ec: e.tensor_mul(
                        out=x_f[:, ec * 512:(ec + 1) * 512],
                        in0=x_f[:, ec * 512:(ec + 1) * 512], in1=ab_f[:]),
                        wait=([("sc", t_ab), ("pe", t_stats)] if ec == 0 else ()), inc=["ve"])
                    tv = em.op("vector", lambda e, ec=ec, bi=bi: e.tensor_sub(
                        out=xb[bi][:, ec * 512:(ec + 1) * 512],
                        in0=x_f[:, ec * 512:(ec + 1) * 512], in1=bb_f[:]),
                        wait=[("ve", tm)], inc=["ve"])
                rel["x_f"] = ("ve", tv)
                rel["p_abb"] = ("ve", tv)
                return tv

            def proj_kq(dst_sc, Wsrc, wbuf, bias_s, pair, t_apply, is_q):
                gs0, gs1 = 2 * pair, 2 * pair + 1
                xb_p = [xb[(pair % 2) * 2], xb[(pair % 2) * 2 + 1]]
                for dc in range(16):
                    wi = dc % 2
                    wl = [rel[f"{wbuf[wi].name}"]] if wbuf[wi].name in rel else []
                    tw = em.dma("gpsimd", lambda e, dc=dc, wi=wi: e.dma_start(
                        out=wbuf[wi][:], in_=Wsrc[dc].rearrange("p a b -> p (a b)")), "dG", wait=wl)
                    for sp2 in range(2):
                        bank = p_kq[sp2]
                        wk0 = [("dG", tw), ("ve", t_apply)]
                        if f"p_kq{sp2}" in rel:
                            wk0.append(rel[f"p_kq{sp2}"])
                        for ec in range(16):
                            tpe = em.op("tensor", lambda e, dc=dc, wi=wi, ec=ec, sp2=sp2, bank=bank, xb_p=xb_p: e.matmul(
                                out=bank[:], lhsT=wbuf[wi][:, ec * 128:(ec + 1) * 128],
                                rhs=xb_p[sp2][:, ec * 512:(ec + 1) * 512],
                                start=(ec == 0), stop=(ec == 15)),
                                wait=(wk0 if ec == 0 else ()),
                                inc=(["pe"] if ec == 15 else []))
                        rel[wbuf[wi].name] = ("pe", tpe)
                        ei = kq_ev_i[0] % 4
                        kq_ev_i[0] += 1
                        wev = [("pe", tpe)]
                        if kq_ev_tok[ei]:
                            wev.append(("dS", kq_ev_tok[ei]))
                        tse = em.op("scalar", lambda e, ei=ei, dc=dc, bank=bank: e.add(
                            out=kq_ev[ei][:], in_=bank[:], add=bias_s[:, dc:dc + 1]),
                            wait=wev, inc=["sc"])
                        rel[f"p_kq{sp2}"] = ("sc", tse)
                        gs = gs0 if sp2 == 0 else gs1
                        if is_q:
                            col0 = (gs - 4) * 512
                        else:
                            col0 = gs * 512
                        kq_ev_tok[ei] = em.dma("sync", lambda e, ei=ei, dc=dc, col0=col0, dst=dst_sc: e.dma_start(
                            out=dst[dc * 128:(dc + 1) * 128, col0:col0 + 512], in_=kq_ev[ei][:]),
                            "dS", wait=[("sc", tse)])

            def proj_v_half(pair, dsh, t_apply):
                xb_p = [xb[(pair % 2) * 2], xb[(pair % 2) * 2 + 1]]
                wv_w = [rel["wv_s"]] if "wv_s" in rel else []
                t_wv = 0
                for ec in range(16):
                    t_wv = em.dma("gpsimd", lambda e, ec=ec, dsh=dsh: e.dma_start(
                        out=wv_s[:, ec * 1024:(ec + 1) * 1024],
                        in_=WvH[ec, :, dsh * 1024:(dsh + 1) * 1024]), "dG",
                        wait=(wv_w if ec == 0 else ()))
                for tc8 in range(8):
                    sp2, tci = tc8 // 4, tc8 % 4
                    kc = pair * 8 + tc8
                    for dsi in range(2):
                        ds = dsh * 2 + dsi
                        bank = p_v[dsi]
                        w0 = [("dG", t_wv), ("ve", t_apply), ("ve", t_bv)]
                        if f"p_v{dsi}" in rel:
                            w0.append(rel[f"p_v{dsi}"])
                        em.op("tensor", lambda e, ds=ds, bank=bank: e.matmul(
                            out=bank[:], lhsT=ones_row[:], rhs=bv_hi[0:1, ds * 512:(ds + 1) * 512],
                            start=True, stop=False), wait=w0)
                        em.op("tensor", lambda e, ds=ds, bank=bank: e.matmul(
                            out=bank[:], lhsT=ones_row[:], rhs=bv_lo[0:1, ds * 512:(ds + 1) * 512],
                            start=False, stop=False))
                        tpe = 0
                        for ec in range(16):
                            tpe = em.op("tensor", lambda e, ec=ec, sp2=sp2, tci=tci, dsi=dsi, bank=bank, xb_p=xb_p: e.matmul(
                                out=bank[:],
                                lhsT=xb_p[sp2][:, ec * 512 + tci * 128: ec * 512 + (tci + 1) * 128],
                                rhs=wv_s[:, ec * 1024 + dsi * 512: ec * 1024 + (dsi + 1) * 512],
                                start=False, stop=(ec == 15)),
                                inc=(["pe"] if ec == 15 else []))
                        ei = v_ev_i[0] % 4
                        v_ev_i[0] += 1
                        wev = [("pe", tpe)]
                        if v_ev_tok[ei]:
                            wev.append(("dS", v_ev_tok[ei]))
                        tse = em.op("scalar", lambda e, ei=ei, bank=bank: e.copy(
                            out=v_ev[ei][:], in_=bank[:]), wait=wev, inc=["sc"])
                        rel[f"p_v{dsi}"] = ("sc", tse)
                        tdma = 0
                        for hh in range(4):
                            h = ds * 4 + hh
                            tdma = em.dma("sync", lambda e, ei=ei, h=h, kc=kc, hh=hh: e.dma_start(
                                out=v3_sc[h, :, kc, :], in_=v_ev[ei][:, hh * 128:(hh + 1) * 128]),
                                "dS", wait=([("sc", tse)] if hh == 0 else ()))
                        v_ev_tok[ei] = tdma
                rel["wv_s"] = ("pe", CN["pe"])

            for pair in range(4):
                b0, b1_ = (pair % 2) * 2, (pair % 2) * 2 + 1
                ta = xln_span(2 * pair, b0)
                tb = xln_span(2 * pair + 1, b1_)
                proj_kq(kT_sc, WkH, wk_s, bk_s, pair, tb, is_q=False)
                proj_v_half(pair, 0, tb)
                proj_v_half(pair, 1, tb)
                if pair >= 2:
                    proj_kq(qT_sc, WqH, wq_s, bq_s, pair, tb, is_q=True)
                rel[f"xb{b0}"] = rel[f"xb{b1_}"] = ("pe", CN["pe"])

            t_ph1_dma = CN["dS"]
            t_ph1_pe = CN["pe"]
            t_ph1_sc = CN["sc"]
            t_ph1_ve = CN["ve"]

        def finish(write_zero_out):
            if dumps:
                for name, dap in dumps.items():
                    em.dma("sync", lambda e, name=name, dap=dap: e.dma_start(
                        out=dap, in_=scratch[name]), "dS",
                        wait=[("dS", CN["dS"]), ("ve", CN["ve"]), ("sc", CN["sc"]), ("pe", CN["pe"])])
            if write_zero_out:
                zf = sbr("zf", (128, 512), F32)
                em.op("vector", lambda e: e.memset(zf[:], 0.0), inc=["ve"])
                tz = CN["ve"]
                for ec in range(16):
                    for spc in range(4):
                        em.dma("sync", lambda e, ec=ec, spc=spc: e.dma_start(
                            out=outT[ec * 128:(ec + 1) * 128, spc * 512:(spc + 1) * 512], in_=zf[:]),
                            "dS", wait=[("ve", tz), ("dS", CN["dS"])])
            em.op("sync", lambda e: e.nop(), wait=[("dS", CN["dS"])])
            block = root.enter_context(nc.Block())
            em.replay(block)

        if phases < 2:
            finish(True)
            return nc, dumps

        ph23 = ExitStack()
        with ph23:
            sb23 = lambda n, s, d: ph23.enter_context(nc.sbuf_tensor(n, list(s), d))
            ps23 = lambda n, s: ph23.enter_context(nc.psum_tensor(n, list(s), F32))
            alb_s = sb23("alb_s", (128, 512), F32)
            maskb_s = sb23("maskb_s", (128, 2048), BF16)
            kT_h = [sb23(f"kT_h{i}", (128, 4096), BF16) for i in range(2)]
            v_h = [sb23(f"v_h{i}", (128, 4096), BF16) for i in range(2)]
            qT_h = [sb23(f"qT_h{i}", (128, 512), BF16) for i in range(2)]
            qb_b = [sb23(f"qb_b{i}", (1, 512), BF16) for i in range(2)]
            qb_bcb = sb23("qb_bcb", (128, 512), BF16)
            probsT = sb23("probsT", (128, 32 * 512), BF16)
            rb_f = sb23("rb_f", (128, 512), F32)
            UT_sp = sb23("UT_sp", (128, 16 * 512), BF16)
            wo_sb = sb23("wo_sb", (128, 32768), BF16)
            aT_b = sb23("aT_b", (128, 8192), BF16)
            asq_c = sb23("asq_c", (128, 512), BF16)
            x_t = [sb23(f"x_t{i}", (128, 512), F32) for i in range(2)]
            x2_ev = [sb23(f"x2_ev{i}", (128, 512), F32) for i in range(2)]
            p_sc2 = [ps23(f"p_sc2{i}", (128, 512)) for i in range(4)]
            p_pv = [ps23(f"p_pv{i}", (128, 512)) for i in range(2)]
            p_sums = ps23("p_sums", (1, 512))
            p_rb = ps23("p_rb", (128, 512))

            w_bound = [("pe", t_ph1_pe), ("sc", t_ph1_sc), ("ve", t_ph1_ve),
                       ("dS", t_ph1_dma), ("dG", CN["dG"])]
            t_wo = 0
            for hh in range(16):
                t_wo = em.dma("gpsimd", lambda e, hh=hh: e.dma_start(
                    out=wo_sb[:, hh * 2048:(hh + 1) * 2048],
                    in_=WoH[hh].rearrange("p a b -> p (a b)")), "dG",
                    wait=(w_bound if hh == 0 else ()))
            em.op("sync", lambda e: e.nop(), wait=[("pe", t_ph1_pe), ("sc", t_ph1_sc),
                                                   ("ve", t_ph1_ve)])
            t_alb = 0

            bank_rel = [0, 0, 0, 0]
            pv_rel = [0, 0]
            pend = None
            pvb = 0
            sums_rel = 0
            rb_rel = 0
            relh = {}
            probs_rel = 0
            UT_rel = 0
            aT_rel = 0
            abb_rel = 0
            stats_rel = 0
            x2ev_tok = [0, 0]
            xt_tok = [0, 0]
            t_spanload = 0

            pend_ln = None
            emit_lnB_holder = [None]

            def emit_normB():
                nonlocal pend, rb_rel, pv_rel, UT_rel, t_UT_done
                if not pend:
                    return
                hh, hl, pb = pend["h"], pend["hl"], pend["pvb"]
                pend = None
                wrb = [("ve", hl)]
                if rb_rel:
                    wrb.append(rb_rel)
                em.op("tensor", lambda e: e.matmul(
                    out=p_rb[:], lhsT=ones_row[:], rhs=a_hi[:], start=True, stop=False),
                    wait=wrb)
                t_rb = em.op("tensor", lambda e: e.matmul(
                    out=p_rb[:], lhsT=ones_row[:], rhs=a_lo[:], start=False, stop=True),
                    inc=["pe"])
                wrf = [("pe", t_rb)]
                if pv_rel[pb]:
                    wrf.append(("ve", pv_rel[pb]))
                t_rbf = em.op("scalar", lambda e: e.copy(out=rb_f[:], in_=p_rb[:]),
                              wait=wrf, inc=["sc"])
                rb_rel = ("sc", t_rbf)
                wmu = [("sc", t_rbf)]
                if UT_rel:
                    wmu.append(("pe", UT_rel))
                t_UT_done = em.op("vector", lambda e, hh=hh, pb=pb: e.tensor_mul(
                    out=UT_sp[:, hh * 512:(hh + 1) * 512],
                    in0=p_pv[pb][:], in1=rb_f[:]),
                    wait=wmu, inc=["ve"])
                pv_rel[pb] = t_UT_done

            def emit_lnB():
                nonlocal pend_ln, aT_rel, abb_rel
                if not pend_ln:
                    return
                t_evac, sp, t_stats = pend_ln["t_evac"], pend_ln["sp"], pend_ln["t_stats"]
                pend_ln = None
                ew = [("sc", bank_rel[2]), ("sc", bank_rel[3])]
                t_ab = ln_rest(p_sc2[2], p_sc2[3], t_evac, extra_wait=ew)
                bank_rel[2] = t_ab
                bank_rel[3] = t_ab
                t_add2 = 0
                for eo in range(16):
                    bi4 = eo % 2
                    t_x = em.dma("sync", lambda e, eo=eo, sp=sp, bi4=bi4: e.dma_start(
                        out=x_t[bi4][:], in_=xTp[eo * 128:(eo + 1) * 128,
                                                 2048 + sp * 512: 2048 + (sp + 1) * 512]),
                        "dS", wait=([("ve", xt_tok[bi4])] if xt_tok[bi4] else []))
                    wm0 = [("sc", t_ab), ("pe", t_stats)]
                    if x2ev_tok[bi4]:
                        wm0.append(("dS", x2ev_tok[bi4]))
                    t_m = em.op("vector", lambda e, eo=eo, bi4=bi4: e.tensor_mul(
                        out=x2_ev[bi4][:],
                        in0=aT_b[:, eo * 512:(eo + 1) * 512], in1=ab_f[:]),
                        wait=wm0, inc=["ve"])
                    t_s2 = em.op("vector", lambda e, eo=eo, bi4=bi4: e.tensor_sub(
                        out=x2_ev[bi4][:], in0=x2_ev[bi4][:], in1=bb_f[:]),
                        wait=[("ve", t_m)], inc=["ve"])
                    wadd = [("ve", t_s2), ("dS", t_x)]
                    t_add2 = em.op("vector", lambda e, eo=eo, bi4=bi4: e.tensor_add(
                        out=x2_ev[bi4][:], in0=x2_ev[bi4][:],
                        in1=x_t[bi4][:]), wait=wadd, inc=["ve"])
                    xt_tok[bi4] = t_add2
                    x2ev_tok[bi4] = em.dma("sync", lambda e, eo=eo, sp=sp, bi4=bi4: e.dma_start(
                        out=x2T_sc[eo * 128:(eo + 1) * 128, sp * 512:(sp + 1) * 512],
                        in_=x2_ev[bi4][:]), "dS", wait=[("ve", t_add2)])
                aT_rel = t_add2
                abb_rel = t_add2
            emit_lnB_holder[0] = emit_lnB

            for sp in range(4):
                NK = 20 + 4 * sp
                d0 = 16 + 4 * sp
                wld0 = [("dS", t_consts)] if sp == 0 else []
                t_alb = em.dma("sync", lambda e, sp=sp: e.dma_start(
                    out=alb_s[:], in_=alb_d[:, sp * 512:(sp + 1) * 512]), "dS",
                    wait=wld0 + ([("sc", max(bank_rel))] if max(bank_rel) else []))
                t_msk = em.dma("sync", lambda e, sp=sp: e.dma_start(
                    out=maskb_s[:], in_=maskb_d[:, sp * 2048:(sp + 1) * 2048]), "dS",
                    wait=([("pe", probs_rel)] if probs_rel else []))
                t_UT_done = 0
                for h in range(16):
                    hb = h % 2
                    wld = [relh[hb]] if hb in relh else [("dS", t_msk)]
                    t_k = em.dma("sync", lambda e, h=h, hb=hb, NK=NK: e.dma_start(
                        out=kT_h[hb][:, 0:NK * 128],
                        in_=kT_sc[h * 128:(h + 1) * 128, 0:NK * 128]), "dS", wait=wld)
                    t_v = em.dma("sync", lambda e, h=h, hb=hb, NK=NK: e.dma_start(
                        out=v_h[hb][:, 0:NK * 128],
                        in_=v3_sc[h].rearrange("p a b -> p (a b)")[:, 0:NK * 128]), "dS")
                    t_q = em.dma("sync", lambda e, h=h, hb=hb, sp=sp: e.dma_start(
                        out=qT_h[hb][:], in_=qT_sc[h * 128:(h + 1) * 128, sp * 512:(sp + 1) * 512]),
                        "dS")
                    t_qb = em.dma("sync", lambda e, h=h, hb=hb, sp=sp: e.dma_start(
                        out=qb_b[hb][:], in_=qb_d[h:h + 1, sp * 512:(sp + 1) * 512]), "dS",
                        serialize=True)
                    use_qb = True
                    if use_qb:
                        wqb = [("dS", t_qb), ("ve", t_ones)]
                        if rb_rel:
                            wqb.append(rb_rel)
                        t_qm = em.op("tensor", lambda e, hb=hb: e.matmul(
                            out=p_rb[:], lhsT=ones_row[:], rhs=qb_b[hb][:],
                            start=True, stop=True), wait=wqb, inc=["pe"])
                        wqe = [("pe", t_qm)]
                        if probs_rel:
                            wqe.append(("pe", probs_rel))
                        t_qe = em.op("scalar", lambda e: e.copy(
                            out=qb_bcb[:], in_=p_rb[:]), wait=wqe, inc=["sc"])
                        rb_rel = ("sc", t_qe)
                    t_exp = 0
                    for kc in range(NK):
                        bi2 = kc % 4
                        bank = p_sc2[bi2]
                        w0 = [("dS", t_q)]
                        if bank_rel[bi2]:
                            w0.append(("sc", bank_rel[bi2]))
                        if kc == 0 and probs_rel:
                            w0.append(("pe", probs_rel))
                        diag = (d0 <= kc < d0 + 4)
                        started = False
                        if use_qb:
                            em.op("tensor", lambda e, bank=bank: e.matmul(
                                out=bank[:], lhsT=o128[:], rhs=qb_bcb[:],
                                start=True, stop=False),
                                wait=(w0 + ([("sc", t_qe)] if kc == 0 else [])))
                            started = True
                        if diag:
                            em.op("tensor", lambda e, kc=kc, d0=d0, bank=bank: e.matmul(
                                out=bank[:], lhsT=id_b[:],
                                rhs=maskb_s[:, (kc - d0) * 512:(kc - d0 + 1) * 512],
                                start=(not started), stop=False),
                                wait=(() if started else w0))
                            started = True
                        t_pe = em.op("tensor", lambda e, kc=kc, hb=hb, bank=bank, started=started: e.matmul(
                            out=bank[:], lhsT=kT_h[hb][:, kc * 128:(kc + 1) * 128],
                            rhs=qT_h[hb][:],
                            start=(not started), stop=True),
                            wait=(() if started else w0), inc=["pe"])
                        wex = [("pe", t_pe), ("dS", t_alb)]
                        t_exp = em.op("scalar", lambda e, kc=kc, sp=sp, h=h, bank=bank: e.activation(
                            out=probsT[:, kc * 512:(kc + 1) * 512], in_=bank[:], func=AF.Exp,
                            scale=INV_SQRT_D,
                            bias=alb_s[:, kc * 16 + h:kc * 16 + h + 1]),
                            wait=wex, inc=["sc"])
                        bank_rel[bi2] = t_exp
                    if dbg4 and (sp, h) == DBG_SPH:
                        em.dma("sync", lambda e: e.dma_start(out=dbg_t["p0"], in_=probsT[:]),
                               "dS", wait=[("sc", t_exp)])
                        em.dma("sync", lambda e: e.dma_start(out=dbg_t["qbc"], in_=qb_bcb[:]),
                               "dS", wait=[("sc", t_exp)])
                    emit_normB()
                    if h == 0 and emit_lnB_holder[0] is not None:
                        emit_lnB_holder[0]()
                    pvb = h % 2
                    t_pvs = 0
                    for kc in range(NK):
                        w1l = []
                        if kc == 0:
                            w1l = [("sc", t_exp)]
                            if pv_rel[pvb]:
                                w1l.append(("ve", pv_rel[pvb]))
                        em.op("tensor", lambda e, kc=kc, hb=hb, NK=NK, pvb=pvb: e.matmul(
                            out=p_pv[pvb][:], lhsT=v_h[hb][:, kc * 128:(kc + 1) * 128],
                            rhs=probsT[:, kc * 512:(kc + 1) * 512],
                            start=(kc == 0), stop=(kc == NK - 1)), wait=w1l)
                        w2l = []
                        if kc == 0 and sums_rel:
                            w2l = [sums_rel]
                        t_pvs = em.op("tensor", lambda e, kc=kc, NK=NK: e.matmul(
                            out=p_sums[:], lhsT=ones_col[:],
                            rhs=probsT[:, kc * 512:(kc + 1) * 512],
                            start=(kc == 0), stop=(kc == NK - 1)), wait=w2l,
                            inc=(["pe"] if kc == NK - 1 else []))
                    probs_rel = t_pvs
                    relh[hb] = ("pe", t_pvs)
                    t_r = em.op("vector", lambda e: e.reciprocal(out=a_row[:], in_=p_sums[:]),
                                wait=[("pe", t_pvs)], inc=["ve"])
                    sums_rel = ("ve", t_r)
                    t_hl = hilo_split(a_row, a_hi, a_lo, t_row, [("ve", t_r)])
                    pend = {"hl": t_hl, "h": h, "pvb": pvb}

                emit_normB()
                t_a_tok = [0] * 16
                for eo in range(16):
                    bi3 = eo % 2
                    bank = p_sc2[bi3]
                    w0 = [("dG", t_wo), ("ve", t_UT_done), ("sc", bank_rel[bi3])]
                    t_a = 0
                    for h16 in range(16):
                        t_a = em.op("tensor", lambda e, h16=h16, eo=eo, bank=bank: e.matmul(
                            out=bank[:], lhsT=wo_sb[:, (h16 * 16 + eo) * 128:(h16 * 16 + eo + 1) * 128],
                            rhs=UT_sp[:, h16 * 512:(h16 + 1) * 512],
                            start=(h16 == 0), stop=(h16 == 15)),
                            wait=(w0 if h16 == 0 else ()),
                            inc=(["pe"] if h16 == 15 else []))
                    wev = [("pe", t_a)]
                    if eo == 0 and aT_rel:
                        wev.append(("ve", aT_rel))
                    t_ev = em.op("scalar", lambda e, eo=eo, bank=bank: e.add(
                        out=aT_b[:, eo * 512:(eo + 1) * 512], in_=bank[:],
                        add=bo_s[:, eo:eo + 1]), wait=wev, inc=["sc"])
                    bank_rel[bi3] = t_ev
                    t_a_tok[eo] = (t_a, t_ev)
                UT_rel = t_a_tok[15][0]
                abc_rel = 0
                t_stats = 0
                for eo in range(16):
                    t_ev = t_a_tok[eo][1]
                    wc = [("sc", t_ev)]
                    if abc_rel:
                        wc.append(("pe", abc_rel))
                    t_s = em.op("scalar", lambda e, eo=eo: e.activation(
                        out=asq_c[:], in_=aT_b[:, eo * 512:(eo + 1) * 512], func=AF.Square),
                        wait=wc, inc=["sc"])
                    wm1 = [("sc", t_ev)]
                    if eo == 0:
                        wm1.append(sums_rel)
                        if stats_rel:
                            wm1.append(("sc", stats_rel))
                    em.op("tensor", lambda e, eo=eo: e.matmul(
                        out=p_sums[:], lhsT=ones_col[:], rhs=aT_b[:, eo * 512:(eo + 1) * 512],
                        start=(eo == 0), stop=(eo == 15)), wait=wm1)
                    wm2 = [("sc", t_s)]
                    if eo == 0:
                        wm2.append(rb_rel)
                    t_stats = em.op("tensor", lambda e, eo=eo: e.matmul(
                        out=p_rb[0:1, :], lhsT=ones_col[:], rhs=asq_c[:],
                        start=(eo == 0), stop=(eo == 15)), wait=wm2,
                        inc=["pe"])
                    abc_rel = t_stats
                t_evac = ln_evac(p_sums, p_rb[0:1, :], t_stats, 2048.0,
                                 extra_wait=([("ve", abb_rel)] if abb_rel else []))
                stats_rel = t_evac
                sums_rel = ("sc", t_evac)
                rb_rel = ("sc", t_evac)
                pend_ln = {"t_evac": t_evac, "sp": sp, "t_stats": t_stats}


            if emit_lnB_holder[0] is not None:
                emit_lnB_holder[0]()
            t_ph3_pe = CN["pe"]; t_ph3_sc = CN["sc"]; t_ph3_ve = CN["ve"]
            t_ph3_dS = CN["dS"]; t_ph3_dG = CN["dG"]

        if phases < 4:
            finish(True)
            return nc, dumps

        ph4 = ExitStack()
        with ph4:
            sb4 = lambda n, s, d: ph4.enter_context(nc.sbuf_tensor(n, list(s), d))
            xfT = sb4("xfT", (128, 16 * 1024), BF16)
            y_s = sb4("y_s", (128, 16 * 1024), F32)
            g_s = sb4("g_s", (128, NHC * 1024), BF16)
            cwT = sb4("cwT", (16, 1024), F32)
            c_hi16 = sb4("c_hi16", (16, 1024), BF16)
            c_lo16 = sb4("c_lo16", (16, 1024), BF16)
            c_hi_e = sb4("c_hi_e", (1, 1024), BF16)
            c_lo_e = sb4("c_lo_e", (1, 1024), BF16)
            cb_f = [sb4(f"cb_f{i}", (128, 512), F32) for i in range(2)]
            w1_s = [sb4(f"w1_s{i}", (128, 2048), BF16) for i in range(2)]
            w3_s = [sb4(f"w3_s{i}", (128, 2048), BF16) for i in range(2)]
            w2_s = [sb4(f"w2_s{i}", (128, NHC * 128), BF16) for i in range(4)]
            b2row_b = sb4("b2row_b", (1, 2048), BF16)
            silu_f = sb4("silu_f", (128, 512), F32)
            tg_f = sb4("tg_f", (128, 512), F32)
            scores_sb = sb4("scores_sb", (128, 128), F32)
            gmx = sb4("gmx", (128, 1), F32)
            ge_f = sb4("ge_f", (128, 16), F32)
            gsum = sb4("gsum", (128, 1), F32)
            gpr = sb4("gpr", (128, 16), F32)
            gbi = sb4("gbi", (128, 16), F32)
            gm8 = sb4("gm8", (128, 8), F32)
            gmask = sb4("gmask", (128, 16), F32)
            gcw = sb4("gcw", (128, 16), F32)
            out_ev = [sb4(f"out_ev{i}", (128, 512), F32) for i in range(2)]
            x2_t = [sb4(f"x2_t{i}", (128, 512), F32) for i in range(2)]
            y_addr = nc.lookup_mloc(y_s).addr
            x2_f = nc.alloc_sbuf_tensor_at("x2_f_alias", [128, 8192], F32, offset=y_addr)
            x2b = nc.alloc_sbuf_tensor_at("x2b_alias", [128, 8192], BF16, offset=y_addr + 32768)
            x2sq = nc.alloc_sbuf_tensor_at("x2sq_alias", [128, 8192], BF16, offset=y_addr + 49152)

            join()
            bnd4 = []
            first4 = [True]
            out_tok = [0, 0]
            x2t_tok = [0, 0]

            for hf in range(2):
                ph4a = ExitStack()
                with ph4a:
                    ps4a = lambda n, s: ph4a.enter_context(nc.psum_tensor(f"{n}_h{hf}", list(s), F32))
                    p_s14 = ps4a("p_s14", (1, 512))
                    p_s24 = ps4a("p_s24", (1, 512))
                    p_ab4 = ps4a("p_ab4", (128, 512))
                    p_bb4 = ps4a("p_bb4", (128, 512))
                    p_g4 = ps4a("p_g4", (128, 16))
                    rel4 = {}
                    pg_rel = 0
                    t_sgate = 0
                    for sp2 in range(2):
                        col0 = hf * 1024 + sp2 * 512
                        tx = 0
                        for ec in range(16):
                            w0 = list(bnd4) if first4[0] else (
                                [rel4["x2_f"]] if ("x2_f" in rel4 and ec == 0) else [])
                            first4[0] = False
                            tx = em.dma("sync", lambda e, ec=ec, col0=col0: e.dma_start(
                                out=x2_f[:, ec * 512:(ec + 1) * 512],
                                in_=x2T_sc[ec * 128:(ec + 1) * 128, col0:col0 + 512]),
                                "dS", wait=w0)
                        wc = [("dS", tx)]
                        if "x2b" in rel4:
                            wc.append(rel4["x2b"])
                        t_cast = em.op("vector", lambda e: e.tensor_copy(out=x2b[:], in_=x2_f[:]),
                                       wait=wc, inc=["ve"])
                        wsq = [("dS", tx)] + ([rel4["x2sq"]] if "x2sq" in rel4 else [])
                        t_sq = em.op("scalar", lambda e: e.activation(out=x2sq[:], in_=x2_f[:], func=AF.Square),
                                     wait=wsq, inc=["sc"])
                        wst = [("ve", t_cast)]
                        if "p_st4" in rel4:
                            wst.append(rel4["p_st4"])
                        for ec in range(16):
                            em.op("tensor", lambda e, ec=ec: e.matmul(
                                out=p_s14[:], lhsT=ones_col[:], rhs=x2b[:, ec * 512:(ec + 1) * 512],
                                start=(ec == 0), stop=(ec == 15)), wait=(wst if ec == 0 else ()))
                        t_stats = 0
                        for ec in range(16):
                            t_stats = em.op("tensor", lambda e, ec=ec: e.matmul(
                                out=p_s24[:], lhsT=ones_col[:], rhs=x2sq[:, ec * 512:(ec + 1) * 512],
                                start=(ec == 0), stop=(ec == 15)),
                                wait=([("sc", t_sq)] if ec == 0 else ()),
                                inc=(["pe"] if ec == 15 else []))
                        rel4["x2b"] = ("pe", t_stats)
                        rel4["x2sq"] = ("pe", t_stats)
                        ew = [rel4["abb4"]] if "abb4" in rel4 else []
                        t_ab = ln_rows_and_bcast(p_s14, p_s24, p_ab4, p_bb4, t_stats, 2048.0,
                                                 extra_wait=ew)
                        rel4["p_st4"] = ("sc", t_ab)
                        tv = 0
                        tsub = 0
                        for ec in range(16):
                            tm = em.op("vector", lambda e, ec=ec: e.tensor_mul(
                                out=x2_f[:, ec * 512:(ec + 1) * 512],
                                in0=x2_f[:, ec * 512:(ec + 1) * 512], in1=ab_f[:]),
                                wait=([("sc", t_ab), ("pe", t_stats)] if ec == 0 else ()), inc=["ve"])
                            tsub = em.op("vector", lambda e, ec=ec: e.tensor_sub(
                                out=x2_f[:, ec * 512:(ec + 1) * 512],
                                in0=x2_f[:, ec * 512:(ec + 1) * 512], in1=bb_f[:]),
                                wait=[("ve", tm)], inc=["ve"])
                            tv = em.op("vector", lambda e, ec=ec, sp2=sp2: e.tensor_copy(
                                out=xfT[:, ec * 1024 + sp2 * 512: ec * 1024 + (sp2 + 1) * 512],
                                in_=x2_f[:, ec * 512:(ec + 1) * 512]),
                                wait=[("ve", tsub)], inc=["ve"])
                        rel4["abb4"] = ("ve", tv)
                        for tc4 in range(4):
                            tgm = 0
                            for ec in range(16):
                                wgm = []
                                if ec == 0:
                                    wgm = [("ve", tv), ("dS", t_gw)]
                                    if pg_rel:
                                        wgm.append(("sc", pg_rel))
                                tgm = em.op("tensor", lambda e, ec=ec, tc4=tc4: e.matmul(
                                    out=p_g4[:], lhsT=x2_f[:, ec * 512 + tc4 * 128: ec * 512 + (tc4 + 1) * 128],
                                    rhs=gw_s[:, ec * 16:(ec + 1) * 16],
                                    start=(ec == 0), stop=(ec == 15)),
                                    wait=wgm, inc=(["pe"] if ec == 15 else []))
                            t_sgate = em.op("scalar", lambda e, sp2=sp2, tc4=tc4: e.copy(
                                out=scores_sb[:, (sp2 * 4 + tc4) * 16:(sp2 * 4 + tc4 + 1) * 16],
                                in_=p_g4[:]), wait=[("pe", tgm)], inc=["sc"])
                            pg_rel = t_sgate
                        rel4["x2_f"] = ("pe", tgm)
                    t_xf = tv
                    t_scores = t_sgate
                if dbg4 and hf == 0:
                    em.dma("sync", lambda e: e.dma_start(out=dbg_t["sc0"], in_=scores_sb[:]),
                           "dS", wait=[("sc", t_scores)])
                    em.dma("sync", lambda e: e.dma_start(out=dbg_t["xfT"], in_=xfT[:]),
                           "dS", wait=[("ve", t_xf)])

                ph4b = ExitStack()
                with ph4b:
                    ps4b = lambda n, s: ph4b.enter_context(nc.psum_tensor(f"{n}_h{hf}", list(s), F32))
                    p_cw = ps4b("p_cw", (16, 128))
                    g_prev_ve = 0
                    pcw_rel = 0
                    for tc in range(8):
                        w1g = [("sc", t_scores)]
                        if g_prev_ve:
                            w1g.append(("ve", g_prev_ve))
                        t1 = em.op("vector", lambda e, tc=tc: e.reduce_max(
                            out=gmx[:], in_=scores_sb[:, tc * 16:(tc + 1) * 16], axis=AX, negate=True),
                                   wait=w1g, inc=["ve"])
                        t2 = em.op("scalar", lambda e, tc=tc: e.activation(
                            out=ge_f[:], in_=scores_sb[:, tc * 16:(tc + 1) * 16], func=AF.Exp,
                            bias=gmx[:, 0:1]),
                                   wait=[("ve", t1)], inc=["sc"])
                        t3 = em.op("vector", lambda e: e.reduce_sum(out=gsum[:], in_=ge_f[:], axis=AX),
                                   wait=[("sc", t2)], inc=["ve"])
                        t4 = em.op("vector", lambda e: e.reciprocal(out=gsum[:], in_=gsum[:]),
                                   wait=[("ve", t3)], inc=["ve"])
                        t5 = em.op("vector", lambda e: e.tensor_scalar_mul(
                            out=gpr[:], in0=ge_f[:], scalar1=gsum[:, 0:1]),
                            wait=[("ve", t4)], inc=["ve"])
                        t6 = em.op("vector", lambda e: e.tensor_add(out=gbi[:], in0=gpr[:], in1=gb_s[:]),
                                   wait=[("ve", t5)], inc=["ve"])
                        t7 = em.op("vector", lambda e: e.max(out=gm8[:], in_=gbi[:]),
                                   wait=[("ve", t6)], inc=["ve"])
                        t8 = em.op("vector", lambda e: e.tensor_scalar(
                            out=gmask[:], in0=gbi[:], scalar1=gm8[:, 5:6], scalar2=None, op0=OP.is_ge),
                            wait=[("ve", t7)], inc=["ve"])
                        t9 = em.op("vector", lambda e: e.tensor_mul(out=gcw[:], in0=gpr[:], in1=gmask[:]),
                                   wait=[("ve", t8)], inc=["ve"])
                        g_prev_ve = t9
                        wtr = [("ve", t9), ("dS", t_consts)]
                        if pcw_rel:
                            wtr.append(("sc", pcw_rel))
                        t_tr = em.op("tensor", lambda e: e.transpose(out=p_cw[:], in_=gcw[:], identity=id_f[:]),
                                     wait=wtr, inc=["pe"])
                        pcw_rel = em.op("scalar", lambda e, tc=tc: e.copy(
                            out=cwT[:, tc * 128:(tc + 1) * 128], in_=p_cw[:]),
                            wait=[("pe", t_tr)], inc=["sc"])
                    wchl = [("sc", pcw_rel)]
                    if dbg4 and hf == 0:
                        t_dcw = em.dma("sync", lambda e: e.dma_start(out=dbg_t["cwT"], in_=cwT[:]),
                                       "dS", wait=[("sc", pcw_rel)])
                        wchl.append(("dS", t_dcw))
                    t_chl = hilo_split(cwT, c_hi16, c_lo16, cwT, wchl)

                ph4c = ExitStack()
                with ph4c:
                    ps4c = lambda n, s: ph4c.enter_context(nc.psum_tensor(f"{n}_h{hf}", list(s), F32))
                    p_h1 = [ps4c(f"p_h1_{i}", (128, 512)) for i in range(2)]
                    p_h3 = [ps4c(f"p_h3_{i}", (128, 512)) for i in range(2)]
                    p_y = [ps4c(f"p_y{i}", (128, 512)) for i in range(4)]

                    h1_rel = [0, 0]; h3_rel = [0, 0]; y_rel = [0, 0, 0, 0]
                    cb_rel = [0, 0]; pcb_rel = 0
                    w1_rel = [0, 0]; w3_rel = [0, 0]; w2_rel = [0, 0, 0, 0]
                    b2_tok = 0
                    che_tok = 0
                    g_rel = 0
                    t_cbf = 0

                    for e_i in range(18):
                        is_sh = (e_i >= 16)
                        grp = e_i - 16
                        hc0 = grp * NHC if is_sh else 0
                        if not is_sh or grp == 0:
                            wb2 = [("pe", che_tok)] if che_tok else []
                            b2_tok = em.dma("gpsimd", lambda e, e_i=e_i, is_sh=is_sh: e.dma_start(
                                out=b2row_b[:], in_=(bs2H if is_sh else b2H[e_i:e_i + 1, :])), "dG",
                                wait=wb2, serialize=True)
                        if not is_sh:
                            wch = [("ve", t_chl)]
                            if che_tok:
                                wch.append(("pe", che_tok))
                            t_ch = em.dma("sync", lambda e, e_i=e_i: e.dma_start(
                                out=c_hi_e[:], in_=c_hi16[e_i:e_i + 1, :]), "dS", wait=wch,
                                serialize=True)
                            t_cl = em.dma("sync", lambda e, e_i=e_i: e.dma_start(
                                out=c_lo_e[:], in_=c_lo16[e_i:e_i + 1, :]), "dS")
                            for sp2 in range(2):
                                wcb = [("dS", t_cl)]
                                if pcb_rel:
                                    wcb.append(("sc", pcb_rel))
                                if h1_rel[0]:
                                    wcb += [("sc", h1_rel[0]), ("sc", h1_rel[1])]
                                em.op("tensor", lambda e, sp2=sp2: e.matmul(
                                    out=p_h1[0][:], lhsT=ones_row[:],
                                    rhs=c_hi_e[0:1, sp2 * 512:(sp2 + 1) * 512],
                                    start=True, stop=False), wait=wcb)
                                t_cbm = em.op("tensor", lambda e, sp2=sp2: e.matmul(
                                    out=p_h1[0][:], lhsT=ones_row[:],
                                    rhs=c_lo_e[0:1, sp2 * 512:(sp2 + 1) * 512],
                                    start=False, stop=True), inc=["pe"])
                                wcf = [("pe", t_cbm)]
                                if cb_rel[sp2]:
                                    wcf.append(("ve", cb_rel[sp2]))
                                pcb_rel = em.op("scalar", lambda e, sp2=sp2: e.copy(
                                    out=cb_f[sp2][:], in_=p_h1[0][:, 0:512]), wait=wcf, inc=["sc"])
                                h1_rel[0] = pcb_rel
                            t_cbf = pcb_rel
                        t_gv = 0
                        for hci in range(NHC):
                            hc = hc0 + hci
                            wi = hci % 2
                            W1s = ws1H if is_sh else w1H[e_i]
                            W3s = ws3H if is_sh else w3H[e_i]
                            ww = [("pe", w1_rel[wi])] if w1_rel[wi] else []
                            w1_tok = em.dma("gpsimd", lambda e, hc=hc, wi=wi, W1s=W1s: e.dma_start(
                                out=w1_s[wi][:], in_=W1s[hc].rearrange("p a b -> p (a b)")), "dG",
                                wait=ww)
                            ww3 = [("pe", w3_rel[wi])] if w3_rel[wi] else []
                            w3_tok = em.dma("gpsimd", lambda e, hc=hc, wi=wi, W3s=W3s: e.dma_start(
                                out=w3_s[wi][:], in_=W3s[hc].rearrange("p a b -> p (a b)")), "dG",
                                wait=ww3)
                            wh1 = [("dG", w1_tok), ("ve", t_xf)]
                            if h1_rel[0]:
                                wh1 += [("sc", h1_rel[0]), ("sc", h1_rel[1])]
                            t_h1m = 0
                            for ec in range(16):
                                for sp2 in range(2):
                                    t_h1m = em.op("tensor", lambda e, ec=ec, sp2=sp2, wi=wi: e.matmul(
                                        out=p_h1[sp2][:], lhsT=w1_s[wi][:, ec * 128:(ec + 1) * 128],
                                        rhs=xfT[:, ec * 1024 + sp2 * 512: ec * 1024 + (sp2 + 1) * 512],
                                        start=(ec == 0), stop=(ec == 15)),
                                        wait=(wh1 if (ec == 0 and sp2 == 0) else ()),
                                        inc=(["pe"] if (ec == 15 and sp2 == 1) else []))
                            w1_rel[wi] = t_h1m
                            wh3 = [("dG", w3_tok)]
                            if h3_rel[0]:
                                wh3 += [("ve", h3_rel[0]), ("ve", h3_rel[1])]
                            t_h3m = 0
                            for ec in range(16):
                                for sp2 in range(2):
                                    t_h3m = em.op("tensor", lambda e, ec=ec, sp2=sp2, wi=wi: e.matmul(
                                        out=p_h3[sp2][:], lhsT=w3_s[wi][:, ec * 128:(ec + 1) * 128],
                                        rhs=xfT[:, ec * 1024 + sp2 * 512: ec * 1024 + (sp2 + 1) * 512],
                                        start=(ec == 0), stop=(ec == 15)),
                                        wait=(wh3 if (ec == 0 and sp2 == 0) else ()),
                                        inc=(["pe"] if (ec == 15 and sp2 == 1) else []))
                            w3_rel[wi] = t_h3m
                            for sp2 in range(2):
                                bcol = (bs1_s[:, hc:hc + 1] if is_sh
                                        else b1_s[:, e_i * NHC + hc: e_i * NHC + hc + 1])
                                b3col = (bs3_s[:, hc:hc + 1] if is_sh
                                         else b3_s[:, e_i * NHC + hc: e_i * NHC + hc + 1])
                                t_si = em.op("scalar", lambda e, sp2=sp2, bcol=bcol: e.activation(
                                    out=silu_f[:], in_=p_h1[sp2][:], func=AF.Silu, bias=bcol),
                                    wait=[("pe", t_h1m)] + ([("ve", t_gv)] if t_gv else []),
                                    inc=["sc"])
                                h1_rel[sp2] = t_si
                                wg1 = [("sc", t_si), ("pe", t_h3m)]
                                t_g1 = em.op("vector", lambda e, sp2=sp2, b3col=b3col: e.scalar_tensor_tensor(
                                    out=tg_f[:], in0=p_h3[sp2][:], scalar=b3col, in1=silu_f[:],
                                    op0=OP.add, op1=OP.mult), wait=wg1, inc=["ve"])
                                h3_rel[sp2] = t_g1
                                wg2 = [("ve", t_g1)]
                                if hci == 0 and sp2 == 0 and g_rel:
                                    wg2.append(("pe", g_rel))
                                if is_sh:
                                    t_gv = em.op("vector", lambda e, hci=hci, sp2=sp2: e.tensor_copy(
                                        out=g_s[:, hci * 1024 + sp2 * 512: hci * 1024 + (sp2 + 1) * 512],
                                        in_=tg_f[:]), wait=wg2, inc=["ve"])
                                else:
                                    t_gv = em.op("vector", lambda e, hci=hci, sp2=sp2: e.tensor_mul(
                                        out=g_s[:, hci * 1024 + sp2 * 512: hci * 1024 + (sp2 + 1) * 512],
                                        in0=tg_f[:], in1=cb_f[sp2][:]),
                                        wait=wg2 + ([("sc", t_cbf)] if hci == 0 else []), inc=["ve"])
                        if dbg4 and hf == 0 and e_i == 0:
                            em.dma("sync", lambda e: e.dma_start(out=dbg_t["g0"], in_=g_s[:]),
                                   "dS", wait=[("ve", t_gv)])
                            em.dma("sync", lambda e: e.dma_start(out=dbg_t["cb0"][:, 0:512], in_=cb_f[0][:]),
                                   "dS", wait=[("ve", t_gv)])
                            em.dma("sync", lambda e: e.dma_start(out=dbg_t["cb0"][:, 512:1024], in_=cb_f[1][:]),
                                   "dS", wait=[("ve", t_gv)])
                        if not is_sh:
                            cb_rel[0] = cb_rel[1] = t_gv
                        W2s = ws2H if is_sh else w2H[e_i]
                        t_ym = 0
                        for dc in range(16):
                            wi = dc % 4
                            ww2 = [("pe", w2_rel[wi])] if w2_rel[wi] else []
                            w2_tok = em.dma("gpsimd", lambda e, dc=dc, wi=wi, W2s=W2s, hc0=hc0: e.dma_start(
                                out=w2_s[wi][:],
                                in_=W2s[dc, :, hc0:hc0 + NHC, :].rearrange("p a b -> p (a b)")), "dG", wait=ww2)
                            for sp2 in range(2):
                                yb_i = (dc % 2) * 2 + sp2
                                wy0 = [("dG", w2_tok), ("ve", t_gv)]
                                if y_rel[yb_i]:
                                    wy0.append(("ve", y_rel[yb_i]))
                                if is_sh and grp == 1:
                                    pass
                                else:
                                    wy0b = wy0 + ([("dG", b2_tok)] if b2_tok else [])
                                    rhs_c = (ones_q if is_sh else c_hi_e)
                                    em.op("tensor", lambda e, dc=dc, sp2=sp2, rhs_c=rhs_c, is_sh=is_sh, yb_i=yb_i: e.matmul(
                                        out=p_y[yb_i][:],
                                        lhsT=b2row_b[0:1, dc * 128:(dc + 1) * 128],
                                        rhs=(rhs_c[0:1, 0:512] if is_sh else rhs_c[0:1, sp2 * 512:(sp2 + 1) * 512]),
                                        start=True, stop=False), wait=wy0b)
                                for hci in range(NHC):
                                    t_ym = em.op("tensor", lambda e, hci=hci, sp2=sp2, wi=wi, is_sh=is_sh, grp=grp, yb_i=yb_i: e.matmul(
                                        out=p_y[yb_i][:], lhsT=w2_s[wi][:, hci * 128:(hci + 1) * 128],
                                        rhs=g_s[:, hci * 1024 + sp2 * 512: hci * 1024 + (sp2 + 1) * 512],
                                        start=(is_sh and grp == 1 and hci == 0),
                                        stop=(hci == NHC - 1)),
                                        wait=(wy0 if (is_sh and grp == 1 and hci == 0) else ()),
                                        inc=(["pe"] if hci == NHC - 1 else []))
                                w2_rel[wi] = t_ym
                                wya = [("pe", t_ym)]
                                if e_i == 0:
                                    t_ya = em.op("vector", lambda e, dc=dc, sp2=sp2, yb_i=yb_i: e.tensor_copy(
                                        out=y_s[:, dc * 1024 + sp2 * 512: dc * 1024 + (sp2 + 1) * 512],
                                        in_=p_y[yb_i][:]), wait=wya, inc=["ve"])
                                else:
                                    t_ya = em.op("vector", lambda e, dc=dc, sp2=sp2, yb_i=yb_i: e.tensor_add(
                                        out=y_s[:, dc * 1024 + sp2 * 512: dc * 1024 + (sp2 + 1) * 512],
                                        in0=y_s[:, dc * 1024 + sp2 * 512: dc * 1024 + (sp2 + 1) * 512],
                                        in1=p_y[yb_i][:]), wait=wya, inc=["ve"])
                                y_rel[yb_i] = t_ya
                        g_rel = t_ym
                        che_tok = t_ym
                        if dbg4 and hf == 0 and e_i == 0:
                            em.dma("sync", lambda e: e.dma_start(out=dbg_t["y0"], in_=y_s[:]),
                                   "dS", wait=[("ve", y_rel[1])])
                    t_y_done = max(y_rel)
                if dbg4 and hf == 0:
                    em.dma("sync", lambda e: e.dma_start(out=dbg_t["y"], in_=y_s[:]),
                           "dS", wait=[("ve", t_y_done)])

                ph4d = ExitStack()
                with ph4d:
                    sb4d = lambda n, s, d: ph4d.enter_context(nc.sbuf_tensor(f"{n}_h{hf}", list(s), d))
                    ps4d = lambda n, s: ph4d.enter_context(nc.psum_tensor(f"{n}_h{hf}", list(s), F32))
                    yb_c = sb4d("yb_c", (128, 512), BF16)
                    ysq_c = sb4d("ysq_c", (128, 512), BF16)
                    p_s1d = ps4d("p_s1d", (1, 512))
                    p_s2d = ps4d("p_s2d", (1, 512))
                    p_abd = ps4d("p_abd", (128, 512))
                    p_bbd = ps4d("p_bbd", (128, 512))
                    ybc_rel = 0
                    statsd_rel = 0
                    abbd_rel = 0
                    for sp2 in range(2):
                        t_stats = 0
                        for dc in range(16):
                            col = dc * 1024 + sp2 * 512
                            wc = [("ve", t_y_done)]
                            if ybc_rel:
                                wc.append(("pe", ybc_rel))
                            t_c = em.op("vector", lambda e, col=col: e.tensor_copy(
                                out=yb_c[:], in_=y_s[:, col:col + 512]), wait=wc, inc=["ve"])
                            t_s = em.op("scalar", lambda e, col=col: e.activation(
                                out=ysq_c[:], in_=y_s[:, col:col + 512], func=AF.Square),
                                wait=wc, inc=["sc"])
                            wm1 = [("ve", t_c)]
                            if dc == 0 and statsd_rel:
                                wm1.append(("sc", statsd_rel))
                            em.op("tensor", lambda e, dc=dc: e.matmul(
                                out=p_s1d[:], lhsT=ones_col[:], rhs=yb_c[:],
                                start=(dc == 0), stop=(dc == 15)), wait=wm1)
                            t_stats = em.op("tensor", lambda e, dc=dc: e.matmul(
                                out=p_s2d[:], lhsT=ones_col[:], rhs=ysq_c[:],
                                start=(dc == 0), stop=(dc == 15)), wait=[("sc", t_s)],
                                inc=["pe"])
                            ybc_rel = t_stats
                        ew = [("ve", abbd_rel)] if abbd_rel else []
                        t_ab = ln_rows_and_bcast(p_s1d, p_s2d, p_abd, p_bbd, t_stats, 2048.0,
                                                 extra_wait=ew)
                        statsd_rel = t_ab
                        t_last = 0
                        for dc in range(16):
                            col = dc * 1024 + sp2 * 512
                            bi4 = dc % 2
                            wx = [("ve", x2t_tok[bi4])] if x2t_tok[bi4] else []
                            t_x2 = em.dma("sync", lambda e, dc=dc, sp2=sp2, hf=hf, bi4=bi4: e.dma_start(
                                out=x2_t[bi4][:], in_=x2T_sc[dc * 128:(dc + 1) * 128,
                                                           hf * 1024 + sp2 * 512: hf * 1024 + (sp2 + 1) * 512]),
                                "dS", wait=wx)
                            t_m = em.op("vector", lambda e, col=col: e.tensor_mul(
                                out=y_s[:, col:col + 512], in0=y_s[:, col:col + 512], in1=ab_f[:]),
                                wait=[("sc", t_ab), ("pe", t_stats)], inc=["ve"])
                            t_s2 = em.op("vector", lambda e, col=col: e.tensor_sub(
                                out=y_s[:, col:col + 512], in0=y_s[:, col:col + 512], in1=bb_f[:]),
                                wait=[("ve", t_m)], inc=["ve"])
                            wadd = [("ve", t_s2), ("dS", t_x2)]
                            if out_tok[bi4]:
                                wadd.append(("dS", out_tok[bi4]))
                            t_add = em.op("vector", lambda e, col=col, bi4=bi4: e.tensor_add(
                                out=out_ev[bi4][:], in0=y_s[:, col:col + 512], in1=x2_t[bi4][:]),
                                wait=wadd, inc=["ve"])
                            x2t_tok[bi4] = t_add
                            out_tok[bi4] = em.dma("sync", lambda e, dc=dc, sp2=sp2, hf=hf, bi4=bi4: e.dma_start(
                                out=outT[dc * 128:(dc + 1) * 128,
                                         hf * 1024 + sp2 * 512: hf * 1024 + (sp2 + 1) * 512],
                                in_=out_ev[bi4][:]), "dS", wait=[("ve", t_add)])
                            t_last = t_add
                        abbd_rel = t_last
                join()
        finish(False)
        return nc, dumps

    return nc, dumps



def kernel(**inputs):
    sh, per_core = host_prep(inputs)
    nc, _ = build_program(phases=4)
    in_maps = []
    for c in range(8):
        m = dict(sh)
        m.update(per_core[c])
        in_maps.append(m)
    res = run_bass_kernel_spmd(nc, in_maps, core_ids=list(range(8)))
    out = np.empty((4, 4096, 2048), dtype=np.float32)
    for c in range(8):
        b, half = c // 2, c % 2
        out[b, half * 2048:(half + 1) * 2048, :] = np.asarray(res.results[c]["outT"]).T
    return out


# version 12
# speedup vs baseline: 1.0410x; 1.0410x over previous
import numpy as np
import ml_dtypes
from contextlib import ExitStack

import concourse.bass as bass
import concourse.mybir as mybir
from concourse.bass_utils import run_bass_kernel_spmd

F32 = mybir.dt.float32
BF16 = mybir.dt.bfloat16
AF = mybir.ActivationFunctionType
OP = mybir.AluOpType
AX = mybir.AxisListType.X

E = 2048
S = 4096
BATCH = 4
H = 16
NE = 16
HD = 1408
NHC = HD // 128
HS = 2816
NSC = HS // 128
INV_SQRT_D = float(1.0 / np.sqrt(128.0))
ALIBI_SHIFT = -20.0
NEG = -1e30
BF = ml_dtypes.bfloat16
DBG_SPH = (0, 0)


def _alibi_slopes(n):
    import math
    def pow2_slopes(n):
        start = 2 ** (-2 ** (-(math.log2(n) - 3)))
        return [start * start ** i for i in range(n)]
    if math.log2(n).is_integer():
        return pow2_slopes(n)
    c = 2 ** math.floor(math.log2(n))
    return pow2_slopes(c) + _alibi_slopes(2 * c)[::2][:n - c]


class Em:

    ENGINES = ("sync", "gpsimd", "vector", "scalar", "tensor")

    def __init__(self, nc):
        self.nc = nc
        self.ops = {e: [] for e in self.ENGINES}
        self.sems = {}
        self.counts = {}
        self.seen = {e: {} for e in self.ENGINES}

    def sem(self, name, ctx):
        s = ctx.enter_context(self.nc.semaphore(name))
        self.sems[name] = s
        self.counts[name] = 0

    def _push(self, eng, waits, fn, incs):
        sems = self.sems
        seen = self.seen[eng]
        ded = {}
        for (w, v) in waits:
            if v > 0 and v > seen.get(w, 0):
                ded[w] = max(ded.get(w, 0), v)
        for w, v in ded.items():
            seen[w] = v
        waits = list(ded.items())
        def run(engine, _w=tuple(waits), _f=fn, _i=tuple(incs)):
            for (w, v) in _w:
                engine.wait_ge(sems[w], v)
            ins = _f(engine)
            for (s, n) in _i:
                ins = ins.then_inc(sems[s], n)

        self.ops[eng].append(run)

    def op(self, eng, fn, wait=(), inc=()):
        incs = []
        for s in inc:
            self.counts[s] += 1
            incs.append((s, 1))
        self._push(eng, list(wait), fn, incs)
        return self.counts[inc[0]] if inc else None

    def dma(self, eng, fn, sem, wait=(), serialize=False):
        waits = list(wait)
        if serialize:
            waits.append((sem, self.counts[sem]))
        self.counts[sem] += 16
        v = self.counts[sem]
        self._push(eng, waits, fn, [(sem, 16)])
        return v

    def replay(self, block):
        ops = self.ops
        @block.sync
        def _(e):
            for f in ops["sync"]: f(e)
        @block.gpsimd
        def _(e):
            for f in ops["gpsimd"]: f(e)
        @block.vector
        def _(e):
            for f in ops["vector"]: f(e)
        @block.scalar
        def _(e):
            for f in ops["scalar"]: f(e)
        @block.tensor
        def _(e):
            for f in ops["tensor"]: f(e)



def host_prep(inp):
    x = np.asarray(inp["L0_attn_x"])
    Wq = np.asarray(inp["L0_attn_Wq"]); bq = np.asarray(inp["L0_attn_bq"])
    Wk = np.asarray(inp["L0_attn_Wk"]); bk = np.asarray(inp["L0_attn_bk"])
    Wv = np.asarray(inp["L0_attn_Wv"]); bv = np.asarray(inp["L0_attn_bv"])
    Wo = np.asarray(inp["L0_attn_Wo"]); bo = np.asarray(inp["L0_attn_bo"])
    gw = np.asarray(inp["L0_ffn_gate_w"]); gb = np.asarray(inp["L0_ffn_gate_b"])
    w1 = np.asarray(inp["L0_ffn_w1"]); b1 = np.asarray(inp["L0_ffn_b1"])
    w2 = np.asarray(inp["L0_ffn_w2"]); b2 = np.asarray(inp["L0_ffn_b2"])
    w3 = np.asarray(inp["L0_ffn_w3"]); b3 = np.asarray(inp["L0_ffn_b3"])
    ws1 = np.asarray(inp["L0_ffn_ws1"]); bs1 = np.asarray(inp["L0_ffn_bs1"])
    ws2 = np.asarray(inp["L0_ffn_ws2"]); bs2 = np.asarray(inp["L0_ffn_bs2"])
    ws3 = np.asarray(inp["L0_ffn_ws3"]); bs3 = np.asarray(inp["L0_ffn_bs3"])

    C = np.ascontiguousarray
    sh = {}
    def sqW(W):
        return C(W.reshape(16, 128, 16, 128).transpose(0, 3, 2, 1))
    sh["WqH"] = sqW(Wq); sh["WkH"] = sqW(Wk)
    sh["WvH"] = C(Wv.T.reshape(16, 128, 2048))
    sh["WoH"] = C(Wo.reshape(16, 128, 16, 128).transpose(2, 3, 0, 1))
    sh["bqH"] = C(bq.reshape(16, 128).T)
    sh["bkH"] = C(bk.reshape(16, 128).T)
    sh["boH"] = C(bo.reshape(16, 128).T)
    sh["bvH"] = C(bv.reshape(1, 2048))
    sh["w1H"] = C(w1.reshape(16, NHC, 128, 16, 128).transpose(0, 1, 4, 3, 2))
    sh["w3H"] = C(w3.reshape(16, NHC, 128, 16, 128).transpose(0, 1, 4, 3, 2))
    sh["w2H"] = C(w2.reshape(16, 16, 128, NHC, 128).transpose(0, 1, 4, 3, 2))
    sh["ws1H"] = C(ws1.reshape(NSC, 128, 16, 128).transpose(0, 3, 2, 1))
    sh["ws3H"] = C(ws3.reshape(NSC, 128, 16, 128).transpose(0, 3, 2, 1))
    sh["ws2H"] = C(ws2.reshape(16, 128, NSC, 128).transpose(0, 3, 2, 1))
    sh["b1H"] = C(b1.reshape(16 * NHC, 128).T)
    sh["b3H"] = C(b3.reshape(16 * NHC, 128).T)
    sh["b2H"] = C(b2.reshape(16, 16 * 128))
    sh["bs1H"] = C(bs1.reshape(NSC, 128).T)
    sh["bs3H"] = C(bs3.reshape(NSC, 128).T)
    sh["bs2H"] = C(bs2.reshape(1, 2048))
    sh["gwH"] = C(gw.T.reshape(16, 128, 16))
    sh["gbH"] = C(np.broadcast_to(gb, (128, 16)))
    sh["identF"] = np.eye(128, dtype=np.float32)
    sh["identB"] = np.eye(128, dtype=BF)

    sh["ws2H"] = C(ws2.reshape(16, 128, NSC, 128).transpose(0, 3, 2, 1))

    slopes = np.asarray(_alibi_slopes(H), dtype=np.float64)
    xT = x.transpose(0, 2, 1)

    per_core = []
    for c in range(8):
        b, half = c // 2, c % 2
        m = {}
        mine = xT[b][:, half * 2048:(half + 1) * 2048]
        partner = xT[b][:, (1 - half) * 2048:(2 - half) * 2048]
        m["xTp"] = C(np.concatenate([partner, mine], axis=1))
        gpos = np.empty(4096, dtype=np.float64)
        gpos[:2048] = np.arange(2048) + (1 - half) * 2048
        gpos[2048:] = np.arange(2048) + half * 2048
        alb = np.full((4, 32, 128, 16), NEG, dtype=np.float64)
        for sp in range(4):
            q_end = half * 2048 + sp * 512 + 512
            for kc in range(32):
                j = gpos[kc * 128:(kc + 1) * 128]
                col = slopes[None, :] * j[:, None] + ALIBI_SHIFT
                alb[sp, kc] = np.where((j < q_end)[:, None], col, NEG)
        m["alb"] = C(alb.transpose(2, 0, 1, 3).reshape(128, 2048).astype(np.float32))
        mypos = np.arange(2048) + half * 2048
        m["qb"] = C((-slopes[:, None] * mypos[None, :] * np.sqrt(128.0)).astype(BF))
        mb = np.zeros((4, 4, 128, 512), dtype=np.float64)
        for sp in range(4):
            qg = half * 2048 + sp * 512 + np.arange(512)
            for j in range(4):
                kc = 16 + 4 * sp + j
                kg = gpos[kc * 128:(kc + 1) * 128]
                mb[sp, j] = np.where(kg[:, None] > qg[None, :], NEG, 0.0)
        m["maskb"] = C(mb.transpose(2, 0, 1, 3).reshape(128, 8192).astype(BF))
        per_core.append(m)
    return sh, per_core



def build_program(phases=4, dump=(), dbg4=False):
    nc = bass.Bass("TRN2", target_bir_lowering=False, debug=False)

    din = {}
    def DI(name, shape, dt=F32):
        din[name] = nc.dram_tensor(name, list(shape), dt, kind="ExternalInput").ap()
        return din[name]

    xTp = DI("xTp", (2048, 4096))
    WqH = DI("WqH", (16, 128, 16, 128)); WkH = DI("WkH", (16, 128, 16, 128))
    WvH = DI("WvH", (16, 128, 2048)); WoH = DI("WoH", (16, 128, 16, 128))
    bqH = DI("bqH", (128, 16)); bkH = DI("bkH", (128, 16)); boH = DI("boH", (128, 16))
    bvH = DI("bvH", (1, 2048))
    alb_d = DI("alb", (128, 2048))
    qb_d = DI("qb", (16, 2048), BF16)
    maskb_d = DI("maskb", (128, 8192), BF16)
    identF = DI("identF", (128, 128))
    identB = DI("identB", (128, 128), BF16)
    w1H = DI("w1H", (16, NHC, 128, 16, 128)); w3H = DI("w3H", (16, NHC, 128, 16, 128))
    w2H = DI("w2H", (16, 16, 128, NHC, 128))
    ws1H = DI("ws1H", (NSC, 128, 16, 128)); ws3H = DI("ws3H", (NSC, 128, 16, 128))
    ws2H = DI("ws2H", (16, 128, NSC, 128))
    b1H = DI("b1H", (128, 16 * NHC)); b3H = DI("b3H", (128, 16 * NHC))
    b2H = DI("b2H", (16, 2048))
    bs1H = DI("bs1H", (128, NSC)); bs3H = DI("bs3H", (128, NSC))
    bs2H = DI("bs2H", (1, 2048))
    gwH = DI("gwH", (16, 128, 16)); gbH = DI("gbH", (128, 16))

    outT = nc.dram_tensor("outT", [2048, 2048], F32, kind="ExternalOutput").ap()
    dbg_t = {}
    if dbg4:
        dbg_t["xfT"] = nc.dram_tensor("dbg_xfT", [128, 16384], BF16, kind="ExternalOutput").ap()
        dbg_t["y"] = nc.dram_tensor("dbg_y", [128, 16384], F32, kind="ExternalOutput").ap()
        dbg_t["cwT"] = nc.dram_tensor("dbg_cwT", [16, 1024], F32, kind="ExternalOutput").ap()
        dbg_t["p0"] = nc.dram_tensor("dbg_p0", [128, 32 * 512], BF16, kind="ExternalOutput").ap()
        dbg_t["qbc"] = nc.dram_tensor("dbg_qbc", [128, 512], BF16, kind="ExternalOutput").ap()
        dbg_t["g0"] = nc.dram_tensor("dbg_g0", [128, NHC * 1024], BF16, kind="ExternalOutput").ap()
        dbg_t["sc0"] = nc.dram_tensor("dbg_sc0", [128, 128], F32, kind="ExternalOutput").ap()
        dbg_t["y0"] = nc.dram_tensor("dbg_y0", [128, 16384], F32, kind="ExternalOutput").ap()
        dbg_t["cb0"] = nc.dram_tensor("dbg_cb0", [128, 1024], F32, kind="ExternalOutput").ap()

    kT_sc = nc.dram_tensor("kT_sc", [2048, 4096], BF16, kind="Internal").ap()
    v3_sc = nc.dram_tensor("v3_sc", [16, 128, 32, 128], BF16, kind="Internal").ap()
    qT_sc = nc.dram_tensor("qT_sc", [2048, 2048], BF16, kind="Internal").ap()
    x2T_sc = nc.dram_tensor("x2T_sc", [2048, 2048], F32, kind="Internal").ap()
    scratch = {"kT_sc": kT_sc, "v3_sc": v3_sc, "qT_sc": qT_sc, "x2T_sc": x2T_sc}
    dumps = {}
    for name in dump:
        t = scratch[name].tensor
        dumps[name] = nc.dram_tensor("dump_" + name, list(t.shape),
                                     t.dtype, kind="ExternalOutput").ap()

    root = ExitStack()
    em = Em(nc)
    CN = em.counts

    with root:
        sbr = lambda n, s, d: root.enter_context(nc.sbuf_tensor(n, list(s), d))
        for s in ("dS", "dG", "pe", "sc", "ve", "sy", "gp", "fin"):
            em.sem(s, root)

        ones_row = sbr("ones_row", (1, 128), BF16)
        ones_col = sbr("ones_col", (128, 1), BF16)
        ones_q = sbr("ones_q", (1, 512), BF16)
        o128 = sbr("o128", (128, 128), BF16)
        id_f = sbr("id_f", (128, 128), F32)
        id_b = sbr("id_b", (128, 128), BF16)
        bq_s = sbr("bq_s", (128, 16), F32)
        bk_s = sbr("bk_s", (128, 16), F32)
        bo_s = sbr("bo_s", (128, 16), F32)
        gb_s = sbr("gb_s", (128, 16), F32)
        gw_s = sbr("gw_s", (128, 256), F32)
        b1_s = sbr("b1_s", (128, 16 * NHC), F32)
        b3_s = sbr("b3_s", (128, 16 * NHC), F32)
        bs1_s = sbr("bs1_s", (128, NSC), F32)
        bs3_s = sbr("bs3_s", (128, NSC), F32)
        eps_s = sbr("eps_s", (1, 1), F32)
        m_row = sbr("m_row", (1, 512), F32)
        v_row = sbr("v_row", (1, 512), F32)
        t_row = sbr("t_row", (1, 512), F32)
        a_row = sbr("a_row", (1, 512), F32)
        b_row = sbr("b_row", (1, 512), F32)
        a_hi = sbr("a_hi", (1, 512), BF16); a_lo = sbr("a_lo", (1, 512), BF16)
        b_hi = sbr("b_hi", (1, 512), BF16); b_lo = sbr("b_lo", (1, 512), BF16)
        ab_f = sbr("ab_f", (128, 512), F32)
        bb_f = sbr("bb_f", (128, 512), F32)

        for nm, dst, src in (("idf", id_f, identF), ("idb", id_b, identB),
                             ("bq", bq_s, bqH), ("bk", bk_s, bkH), ("bo", bo_s, boH),
                             ("gb", gb_s, gbH),
                             ("b1", b1_s, b1H), ("b3", b3_s, b3H),
                             ("bs1", bs1_s, bs1H), ("bs3", bs3_s, bs3H)):
            t_consts = em.dma("sync", lambda e, d=dst, s=src: e.dma_start(out=d[:], in_=s), "dS")
        t_gw = em.dma("sync", lambda e: e.dma_start(
            out=gw_s[:].rearrange("p (a c) -> p a c", a=16), in_=gwH.transpose([1, 0, 2])), "dS")

        em.op("vector", lambda e: e.memset(ones_row[:], 1.0))
        em.op("vector", lambda e: e.memset(ones_col[:], 1.0))
        em.op("vector", lambda e: e.memset(ones_q[:], 1.0))
        em.op("vector", lambda e: e.memset(o128[:], 1.0 / 128.0))
        t_ones = em.op("vector", lambda e: e.memset(eps_s[:], 1e-5), inc=["ve"])

        def hilo_split(row_f, hi_b, lo_b, tmp_f, wait):
            ta = em.op("vector", lambda e: e.tensor_copy(out=hi_b[:], in_=row_f[:]),
                       wait=wait, inc=["ve"])
            tb = em.op("vector", lambda e: e.tensor_tensor(out=tmp_f[:], in0=row_f[:], in1=hi_b[:], op=OP.subtract),
                       wait=[("ve", ta)], inc=["ve"])
            return em.op("vector", lambda e: e.tensor_copy(out=lo_b[:], in_=tmp_f[:]),
                         wait=[("ve", tb)], inc=["ve"])

        def ln_evac(p_sum, p_sq, t_stats, nfeat, extra_wait=()):
            inv = 1.0 / nfeat
            em.op("scalar", lambda e: e.mul(out=m_row[:], in_=p_sum[:], mul=inv),
                  wait=[("pe", t_stats)] + list(extra_wait), inc=["sc"])
            return em.op("scalar", lambda e: e.mul(out=v_row[:], in_=p_sq[:], mul=inv),
                         wait=[("sc", CN["sc"])], inc=["sc"])

        def ln_rest(p_ab, p_bb, t_evac, extra_wait=()):
            t2 = em.op("vector", lambda e: e.tensor_mul(out=t_row[:], in0=m_row[:], in1=m_row[:]),
                       wait=[("sc", t_evac)] + list(extra_wait), inc=["ve"])
            t3 = em.op("vector", lambda e: e.tensor_sub(out=v_row[:], in0=v_row[:], in1=t_row[:]),
                       wait=[("ve", t2)], inc=["ve"])
            t4 = em.op("scalar", lambda e: e.activation(out=a_row[:], in_=v_row[:], func=AF.Ln,
                                                        bias=eps_s[0:1, 0:1]),
                       wait=[("ve", t3)], inc=["sc"])
            t5 = em.op("scalar", lambda e: e.activation(out=a_row[:], in_=a_row[:], func=AF.Exp, scale=-0.5),
                       wait=[("sc", t4)], inc=["sc"])
            t6 = em.op("vector", lambda e: e.tensor_mul(out=b_row[:], in0=m_row[:], in1=a_row[:]),
                       wait=[("sc", t5)], inc=["ve"])
            t7 = em.op("vector", lambda e: e.tensor_copy(out=a_hi[:], in_=a_row[:]),
                       wait=[("ve", t6)], inc=["ve"])
            t8 = em.op("vector", lambda e: e.tensor_sub(out=t_row[:], in0=a_row[:], in1=a_hi[:]),
                       wait=[("ve", t7)], inc=["ve"])
            t9 = em.op("vector", lambda e: e.tensor_copy(out=a_lo[:], in_=t_row[:]),
                       wait=[("ve", t8)], inc=["ve"])
            t10 = em.op("vector", lambda e: e.tensor_copy(out=b_hi[:], in_=b_row[:]),
                        wait=[("ve", t9)], inc=["ve"])
            t11 = em.op("vector", lambda e: e.tensor_sub(out=t_row[:], in0=b_row[:], in1=b_hi[:]),
                        wait=[("ve", t10)], inc=["ve"])
            t12 = em.op("vector", lambda e: e.tensor_copy(out=b_lo[:], in_=t_row[:]),
                        wait=[("ve", t11)], inc=["ve"])
            em.op("tensor", lambda e: e.matmul(out=p_ab[:], lhsT=ones_row[:], rhs=a_hi[:],
                                               start=True, stop=False), wait=[("ve", t9)])
            em.op("tensor", lambda e: e.matmul(out=p_ab[:], lhsT=ones_row[:], rhs=a_lo[:],
                                               start=False, stop=True))
            em.op("tensor", lambda e: e.matmul(out=p_bb[:], lhsT=ones_row[:], rhs=b_hi[:],
                                               start=True, stop=False), wait=[("ve", t12)])
            t_pe = em.op("tensor", lambda e: e.matmul(out=p_bb[:], lhsT=ones_row[:], rhs=b_lo[:],
                                                      start=False, stop=True), inc=["pe"])
            em.op("scalar", lambda e: e.copy(out=ab_f[:], in_=p_ab[:]),
                  wait=[("pe", t_pe)], inc=["sc"])
            t_ab = em.op("scalar", lambda e: e.copy(out=bb_f[:], in_=p_bb[:]),
                         wait=[("sc", CN["sc"])], inc=["sc"])
            return t_ab

        def ln_rows_and_bcast(p_sum, p_sq, p_ab, p_bb, t_stats, nfeat, extra_wait=()):
            inv = 1.0 / nfeat
            em.op("scalar", lambda e: e.mul(out=m_row[:], in_=p_sum[:], mul=inv),
                  wait=[("pe", t_stats)] + list(extra_wait), inc=["sc"])
            t1 = em.op("scalar", lambda e: e.mul(out=v_row[:], in_=p_sq[:], mul=inv),
                       wait=[("sc", CN["sc"])], inc=["sc"])
            t2 = em.op("vector", lambda e: e.tensor_mul(out=t_row[:], in0=m_row[:], in1=m_row[:]),
                       wait=[("sc", t1)], inc=["ve"])
            t3 = em.op("vector", lambda e: e.tensor_sub(out=v_row[:], in0=v_row[:], in1=t_row[:]),
                       wait=[("ve", t2)], inc=["ve"])
            t4 = em.op("scalar", lambda e: e.activation(out=a_row[:], in_=v_row[:], func=AF.Ln,
                                                        bias=eps_s[0:1, 0:1]),
                       wait=[("ve", t3)], inc=["sc"])
            t5 = em.op("scalar", lambda e: e.activation(out=a_row[:], in_=a_row[:], func=AF.Exp, scale=-0.5),
                       wait=[("sc", t4)], inc=["sc"])
            t6 = em.op("vector", lambda e: e.tensor_mul(out=b_row[:], in0=m_row[:], in1=a_row[:]),
                       wait=[("sc", t5)], inc=["ve"])
            t7 = em.op("vector", lambda e: e.tensor_copy(out=a_hi[:], in_=a_row[:]),
                       wait=[("ve", t6)], inc=["ve"])
            t8 = em.op("vector", lambda e: e.tensor_sub(out=t_row[:], in0=a_row[:], in1=a_hi[:]),
                       wait=[("ve", t7)], inc=["ve"])
            t9 = em.op("vector", lambda e: e.tensor_copy(out=a_lo[:], in_=t_row[:]),
                       wait=[("ve", t8)], inc=["ve"])
            t10 = em.op("vector", lambda e: e.tensor_copy(out=b_hi[:], in_=b_row[:]),
                        wait=[("ve", t9)], inc=["ve"])
            t11 = em.op("vector", lambda e: e.tensor_sub(out=t_row[:], in0=b_row[:], in1=b_hi[:]),
                        wait=[("ve", t10)], inc=["ve"])
            t12 = em.op("vector", lambda e: e.tensor_copy(out=b_lo[:], in_=t_row[:]),
                        wait=[("ve", t11)], inc=["ve"])
            em.op("tensor", lambda e: e.matmul(out=p_ab[:], lhsT=ones_row[:], rhs=a_hi[:],
                                               start=True, stop=False), wait=[("ve", t9)])
            em.op("tensor", lambda e: e.matmul(out=p_ab[:], lhsT=ones_row[:], rhs=a_lo[:],
                                               start=False, stop=True))
            em.op("tensor", lambda e: e.matmul(out=p_bb[:], lhsT=ones_row[:], rhs=b_hi[:],
                                               start=True, stop=False), wait=[("ve", t12)])
            t_pe = em.op("tensor", lambda e: e.matmul(out=p_bb[:], lhsT=ones_row[:], rhs=b_lo[:],
                                                      start=False, stop=True), inc=["pe"])
            em.op("scalar", lambda e: e.copy(out=ab_f[:], in_=p_ab[:]),
                  wait=[("pe", t_pe)], inc=["sc"])
            t_ab = em.op("scalar", lambda e: e.copy(out=bb_f[:], in_=p_bb[:]),
                         wait=[("sc", CN["sc"])], inc=["sc"])
            return t_ab

        def join():
            em.op("sync", lambda e: e.nop(), wait=[("dS", CN["dS"])], inc=["sy"])
            em.op("gpsimd", lambda e: e.nop(), wait=[("dG", CN["dG"])], inc=["gp"])
            em.op("vector", lambda e: e.nop(), inc=["ve"])
            em.op("scalar", lambda e: e.nop(), inc=["sc"])
            em.op("tensor", lambda e: e.nop(), inc=["pe"])
            tg = [(s, CN[s]) for s in ("sy", "gp", "ve", "sc", "pe")]
            for e in ("sync", "gpsimd", "vector", "scalar", "tensor"):
                em.op(e, lambda eng: eng.nop(), wait=tg)

        ph1 = ExitStack()
        with ph1:
            sb1 = lambda n, s, d: ph1.enter_context(nc.sbuf_tensor(n, list(s), d))
            ps1 = lambda n, s: ph1.enter_context(nc.psum_tensor(n, list(s), F32))
            x_f = sb1("x_f", (128, 8192), F32)
            xb = [sb1(f"xb{i}", (128, 8192), BF16) for i in range(4)]
            xsq = sb1("xsq", (128, 8192), BF16)
            wk_s = [sb1(f"wk_s{i}", (128, 2048), BF16) for i in range(2)]
            wq_s = [sb1(f"wq_s{i}", (128, 2048), BF16) for i in range(2)]
            wv_s = sb1("wv_s", (128, 16 * 1024), BF16)
            kq_ev = [sb1(f"kq_ev{i}", (128, 512), BF16) for i in range(4)]
            v_ev = [sb1(f"v_ev{i}", (128, 512), BF16) for i in range(4)]
            bv_row = sb1("bv_row", (1, 2048), F32)
            bv_hi = sb1("bv_hi", (1, 2048), BF16)
            bv_lo = sb1("bv_lo", (1, 2048), BF16)

            p_sum = ps1("p_sum", (1, 512))
            p_sq = ps1("p_sq", (1, 512))
            p_ab = ps1("p_ab", (128, 512))
            p_bb = ps1("p_bb", (128, 512))
            p_kq = [ps1(f"p_kq{i}", (128, 512)) for i in range(2)]
            p_v = [ps1(f"p_v{i}", (128, 512)) for i in range(2)]

            t_bvl = em.dma("sync", lambda e: e.dma_start(out=bv_row[:], in_=bvH), "dS",
                           serialize=True)
            t_bv = hilo_split(bv_row, bv_hi, bv_lo, bv_row, [("dS", t_bvl)])

            rel = {}
            kq_ev_tok = [0, 0, 0, 0]
            kq_ev_i = [0]
            v_ev_tok = [0, 0, 0, 0]
            v_ev_i = [0]

            def xln_span(gs, bi):
                w0 = [rel["x_f"]] if "x_f" in rel else []
                tx = 0
                for ec in range(16):
                    tx = em.dma("sync", lambda e, ec=ec, gs=gs: e.dma_start(
                        out=x_f[:, ec * 512:(ec + 1) * 512],
                        in_=xTp[ec * 128:(ec + 1) * 128, gs * 512:(gs + 1) * 512]),
                        "dS", wait=(w0 if ec == 0 else ()))
                wc = [("dS", tx)]
                if f"xb{bi}" in rel:
                    wc.append(rel[f"xb{bi}"])
                if "xsq" in rel:
                    wsq = [rel["xsq"]]
                else:
                    wsq = []
                t_cast = em.op("vector", lambda e, bi=bi: e.tensor_copy(out=xb[bi][:], in_=x_f[:]),
                               wait=wc, inc=["ve"])
                t_sq = em.op("scalar", lambda e: e.activation(out=xsq[:], in_=x_f[:], func=AF.Square),
                             wait=[("dS", tx)] + wsq, inc=["sc"])
                wst = [("ve", t_cast)]
                if "p_stats" in rel:
                    wst.append(rel["p_stats"])
                for ec in range(16):
                    em.op("tensor", lambda e, ec=ec, bi=bi: e.matmul(
                        out=p_sum[:], lhsT=ones_col[:], rhs=xb[bi][:, ec * 512:(ec + 1) * 512],
                        start=(ec == 0), stop=(ec == 15)), wait=(wst if ec == 0 else ()))
                t_stats = 0
                for ec in range(16):
                    t_stats = em.op("tensor", lambda e, ec=ec: e.matmul(
                        out=p_sq[:], lhsT=ones_col[:], rhs=xsq[:, ec * 512:(ec + 1) * 512],
                        start=(ec == 0), stop=(ec == 15)),
                        wait=([("sc", t_sq)] if ec == 0 else ()),
                        inc=(["pe"] if ec == 15 else ()))
                rel["xsq"] = ("pe", t_stats)
                ew = [rel["p_abb"]] if "p_abb" in rel else []
                t_ab = ln_rows_and_bcast(p_sum, p_sq, p_ab, p_bb, t_stats, 2048.0,
                                         extra_wait=ew + [("ve", t_ones)])
                rel["p_stats"] = ("sc", t_ab)
                tv = 0
                for ec in range(16):
                    tm = em.op("vector", lambda e, ec=ec: e.tensor_mul(
                        out=x_f[:, ec * 512:(ec + 1) * 512],
                        in0=x_f[:, ec * 512:(ec + 1) * 512], in1=ab_f[:]),
                        wait=([("sc", t_ab), ("pe", t_stats)] if ec == 0 else ()), inc=["ve"])
                    tv = em.op("vector", lambda e, ec=ec, bi=bi: e.tensor_sub(
                        out=xb[bi][:, ec * 512:(ec + 1) * 512],
                        in0=x_f[:, ec * 512:(ec + 1) * 512], in1=bb_f[:]),
                        wait=[("ve", tm)], inc=["ve"])
                rel["x_f"] = ("ve", tv)
                rel["p_abb"] = ("ve", tv)
                return tv

            def proj_kq(dst_sc, Wsrc, wbuf, bias_s, pair, t_apply, is_q):
                gs0, gs1 = 2 * pair, 2 * pair + 1
                xb_p = [xb[(pair % 2) * 2], xb[(pair % 2) * 2 + 1]]
                for dc in range(16):
                    wi = dc % 2
                    wl = [rel[f"{wbuf[wi].name}"]] if wbuf[wi].name in rel else []
                    tw = em.dma("gpsimd", lambda e, dc=dc, wi=wi: e.dma_start(
                        out=wbuf[wi][:], in_=Wsrc[dc].rearrange("p a b -> p (a b)")), "dG", wait=wl)
                    for sp2 in range(2):
                        bank = p_kq[sp2]
                        wk0 = [("dG", tw), ("ve", t_apply)]
                        if f"p_kq{sp2}" in rel:
                            wk0.append(rel[f"p_kq{sp2}"])
                        for ec in range(16):
                            tpe = em.op("tensor", lambda e, dc=dc, wi=wi, ec=ec, sp2=sp2, bank=bank, xb_p=xb_p: e.matmul(
                                out=bank[:], lhsT=wbuf[wi][:, ec * 128:(ec + 1) * 128],
                                rhs=xb_p[sp2][:, ec * 512:(ec + 1) * 512],
                                start=(ec == 0), stop=(ec == 15)),
                                wait=(wk0 if ec == 0 else ()),
                                inc=(["pe"] if ec == 15 else []))
                        rel[wbuf[wi].name] = ("pe", tpe)
                        ei = kq_ev_i[0] % 4
                        kq_ev_i[0] += 1
                        wev = [("pe", tpe)]
                        if kq_ev_tok[ei]:
                            wev.append(("dS", kq_ev_tok[ei]))
                        tse = em.op("scalar", lambda e, ei=ei, dc=dc, bank=bank: e.add(
                            out=kq_ev[ei][:], in_=bank[:], add=bias_s[:, dc:dc + 1]),
                            wait=wev, inc=["sc"])
                        rel[f"p_kq{sp2}"] = ("sc", tse)
                        gs = gs0 if sp2 == 0 else gs1
                        if is_q:
                            col0 = (gs - 4) * 512
                        else:
                            col0 = gs * 512
                        kq_ev_tok[ei] = em.dma("sync", lambda e, ei=ei, dc=dc, col0=col0, dst=dst_sc: e.dma_start(
                            out=dst[dc * 128:(dc + 1) * 128, col0:col0 + 512], in_=kq_ev[ei][:]),
                            "dS", wait=[("sc", tse)])

            def proj_v_half(pair, dsh, t_apply):
                xb_p = [xb[(pair % 2) * 2], xb[(pair % 2) * 2 + 1]]
                wv_w = [rel["wv_s"]] if "wv_s" in rel else []
                t_wv = 0
                for ec in range(16):
                    t_wv = em.dma("gpsimd", lambda e, ec=ec, dsh=dsh: e.dma_start(
                        out=wv_s[:, ec * 1024:(ec + 1) * 1024],
                        in_=WvH[ec, :, dsh * 1024:(dsh + 1) * 1024]), "dG",
                        wait=(wv_w if ec == 0 else ()))
                for tc8 in range(8):
                    sp2, tci = tc8 // 4, tc8 % 4
                    kc = pair * 8 + tc8
                    for dsi in range(2):
                        ds = dsh * 2 + dsi
                        bank = p_v[dsi]
                        w0 = [("dG", t_wv), ("ve", t_apply), ("ve", t_bv)]
                        if f"p_v{dsi}" in rel:
                            w0.append(rel[f"p_v{dsi}"])
                        em.op("tensor", lambda e, ds=ds, bank=bank: e.matmul(
                            out=bank[:], lhsT=ones_row[:], rhs=bv_hi[0:1, ds * 512:(ds + 1) * 512],
                            start=True, stop=False), wait=w0)
                        em.op("tensor", lambda e, ds=ds, bank=bank: e.matmul(
                            out=bank[:], lhsT=ones_row[:], rhs=bv_lo[0:1, ds * 512:(ds + 1) * 512],
                            start=False, stop=False))
                        tpe = 0
                        for ec in range(16):
                            tpe = em.op("tensor", lambda e, ec=ec, sp2=sp2, tci=tci, dsi=dsi, bank=bank, xb_p=xb_p: e.matmul(
                                out=bank[:],
                                lhsT=xb_p[sp2][:, ec * 512 + tci * 128: ec * 512 + (tci + 1) * 128],
                                rhs=wv_s[:, ec * 1024 + dsi * 512: ec * 1024 + (dsi + 1) * 512],
                                start=False, stop=(ec == 15)),
                                inc=(["pe"] if ec == 15 else []))
                        ei = v_ev_i[0] % 4
                        v_ev_i[0] += 1
                        wev = [("pe", tpe)]
                        if v_ev_tok[ei]:
                            wev.append(("dS", v_ev_tok[ei]))
                        tse = em.op("scalar", lambda e, ei=ei, bank=bank: e.copy(
                            out=v_ev[ei][:], in_=bank[:]), wait=wev, inc=["sc"])
                        rel[f"p_v{dsi}"] = ("sc", tse)
                        tdma = 0
                        for hh in range(4):
                            h = ds * 4 + hh
                            tdma = em.dma("sync", lambda e, ei=ei, h=h, kc=kc, hh=hh: e.dma_start(
                                out=v3_sc[h, :, kc, :], in_=v_ev[ei][:, hh * 128:(hh + 1) * 128]),
                                "dS", wait=([("sc", tse)] if hh == 0 else ()))
                        v_ev_tok[ei] = tdma
                rel["wv_s"] = ("pe", CN["pe"])

            for pair in range(4):
                b0, b1_ = (pair % 2) * 2, (pair % 2) * 2 + 1
                ta = xln_span(2 * pair, b0)
                tb = xln_span(2 * pair + 1, b1_)
                proj_kq(kT_sc, WkH, wk_s, bk_s, pair, tb, is_q=False)
                proj_v_half(pair, 0, tb)
                proj_v_half(pair, 1, tb)
                if pair >= 2:
                    proj_kq(qT_sc, WqH, wq_s, bq_s, pair, tb, is_q=True)
                rel[f"xb{b0}"] = rel[f"xb{b1_}"] = ("pe", CN["pe"])

            t_ph1_dma = CN["dS"]
            t_ph1_pe = CN["pe"]
            t_ph1_sc = CN["sc"]
            t_ph1_ve = CN["ve"]

        def finish(write_zero_out):
            if dumps:
                for name, dap in dumps.items():
                    em.dma("sync", lambda e, name=name, dap=dap: e.dma_start(
                        out=dap, in_=scratch[name]), "dS",
                        wait=[("dS", CN["dS"]), ("ve", CN["ve"]), ("sc", CN["sc"]), ("pe", CN["pe"])])
            if write_zero_out:
                zf = sbr("zf", (128, 512), F32)
                em.op("vector", lambda e: e.memset(zf[:], 0.0), inc=["ve"])
                tz = CN["ve"]
                for ec in range(16):
                    for spc in range(4):
                        em.dma("sync", lambda e, ec=ec, spc=spc: e.dma_start(
                            out=outT[ec * 128:(ec + 1) * 128, spc * 512:(spc + 1) * 512], in_=zf[:]),
                            "dS", wait=[("ve", tz), ("dS", CN["dS"])])
            em.op("sync", lambda e: e.nop(), wait=[("dS", CN["dS"])])
            block = root.enter_context(nc.Block())
            em.replay(block)

        if phases < 2:
            finish(True)
            return nc, dumps

        ph23 = ExitStack()
        with ph23:
            sb23 = lambda n, s, d: ph23.enter_context(nc.sbuf_tensor(n, list(s), d))
            ps23 = lambda n, s: ph23.enter_context(nc.psum_tensor(n, list(s), F32))
            alb_s = sb23("alb_s", (128, 512), F32)
            maskb_s = sb23("maskb_s", (128, 2048), BF16)
            kT_h = [sb23(f"kT_h{i}", (128, 4096), BF16) for i in range(2)]
            v_h = [sb23(f"v_h{i}", (128, 4096), BF16) for i in range(2)]
            qT_h = [sb23(f"qT_h{i}", (128, 512), BF16) for i in range(2)]
            qb_b = [sb23(f"qb_b{i}", (1, 512), BF16) for i in range(2)]
            qb_bcb = sb23("qb_bcb", (128, 512), BF16)
            probsT = sb23("probsT", (128, 32 * 512), BF16)
            rb_f = sb23("rb_f", (128, 512), F32)
            UT_sp = sb23("UT_sp", (128, 16 * 512), BF16)
            wo_sb = sb23("wo_sb", (128, 32768), BF16)
            aT_b = sb23("aT_b", (128, 8192), BF16)
            asq_c = sb23("asq_c", (128, 512), BF16)
            x_t = [sb23(f"x_t{i}", (128, 512), F32) for i in range(2)]
            x2_ev = [sb23(f"x2_ev{i}", (128, 512), F32) for i in range(2)]
            p_sc2 = [ps23(f"p_sc2{i}", (128, 512)) for i in range(4)]
            p_pv = [ps23(f"p_pv{i}", (128, 512)) for i in range(2)]
            p_sums = ps23("p_sums", (1, 512))
            p_rb = ps23("p_rb", (128, 512))

            w_bound = [("pe", t_ph1_pe), ("sc", t_ph1_sc), ("ve", t_ph1_ve),
                       ("dS", t_ph1_dma), ("dG", CN["dG"])]
            t_wo = 0
            for hh in range(16):
                t_wo = em.dma("gpsimd", lambda e, hh=hh: e.dma_start(
                    out=wo_sb[:, hh * 2048:(hh + 1) * 2048],
                    in_=WoH[hh].rearrange("p a b -> p (a b)")), "dG",
                    wait=(w_bound if hh == 0 else ()))
            em.op("sync", lambda e: e.nop(), wait=[("pe", t_ph1_pe), ("sc", t_ph1_sc),
                                                   ("ve", t_ph1_ve)])
            t_alb = 0

            bank_rel = [0, 0, 0, 0]
            pv_rel = [0, 0]
            pend = None
            pvb = 0
            sums_rel = 0
            rb_rel = 0
            relh = {}
            probs_rel = 0
            UT_rel = 0
            aT_rel = 0
            abb_rel = 0
            stats_rel = 0
            x2ev_tok = [0, 0]
            xt_tok = [0, 0]
            t_spanload = 0

            pend_ln = None
            emit_lnB_holder = [None]

            def emit_normB():
                nonlocal pend, rb_rel, pv_rel, UT_rel, t_UT_done
                if not pend:
                    return
                hh, hl, pb = pend["h"], pend["hl"], pend["pvb"]
                pend = None
                wrb = [("ve", hl)]
                if rb_rel:
                    wrb.append(rb_rel)
                em.op("tensor", lambda e: e.matmul(
                    out=p_rb[:], lhsT=ones_row[:], rhs=a_hi[:], start=True, stop=False),
                    wait=wrb)
                t_rb = em.op("tensor", lambda e: e.matmul(
                    out=p_rb[:], lhsT=ones_row[:], rhs=a_lo[:], start=False, stop=True),
                    inc=["pe"])
                wrf = [("pe", t_rb)]
                if pv_rel[pb]:
                    wrf.append(("ve", pv_rel[pb]))
                t_rbf = em.op("scalar", lambda e: e.copy(out=rb_f[:], in_=p_rb[:]),
                              wait=wrf, inc=["sc"])
                rb_rel = ("sc", t_rbf)
                wmu = [("sc", t_rbf)]
                if UT_rel:
                    wmu.append(("pe", UT_rel))
                t_UT_done = em.op("vector", lambda e, hh=hh, pb=pb: e.tensor_mul(
                    out=UT_sp[:, hh * 512:(hh + 1) * 512],
                    in0=p_pv[pb][:], in1=rb_f[:]),
                    wait=wmu, inc=["ve"])
                pv_rel[pb] = t_UT_done

            def emit_lnB():
                nonlocal pend_ln, aT_rel, abb_rel
                if not pend_ln:
                    return
                t_evac, sp, t_stats = pend_ln["t_evac"], pend_ln["sp"], pend_ln["t_stats"]
                pend_ln = None
                ew = [("sc", bank_rel[2]), ("sc", bank_rel[3])]
                t_ab = ln_rest(p_sc2[2], p_sc2[3], t_evac, extra_wait=ew)
                bank_rel[2] = t_ab
                bank_rel[3] = t_ab
                t_add2 = 0
                for eo in range(16):
                    bi4 = eo % 2
                    t_x = em.dma("sync", lambda e, eo=eo, sp=sp, bi4=bi4: e.dma_start(
                        out=x_t[bi4][:], in_=xTp[eo * 128:(eo + 1) * 128,
                                                 2048 + sp * 512: 2048 + (sp + 1) * 512]),
                        "dS", wait=([("ve", xt_tok[bi4])] if xt_tok[bi4] else []))
                    wm0 = [("sc", t_ab), ("pe", t_stats)]
                    if x2ev_tok[bi4]:
                        wm0.append(("dS", x2ev_tok[bi4]))
                    t_m = em.op("vector", lambda e, eo=eo, bi4=bi4: e.tensor_mul(
                        out=x2_ev[bi4][:],
                        in0=aT_b[:, eo * 512:(eo + 1) * 512], in1=ab_f[:]),
                        wait=wm0, inc=["ve"])
                    t_s2 = em.op("vector", lambda e, eo=eo, bi4=bi4: e.tensor_sub(
                        out=x2_ev[bi4][:], in0=x2_ev[bi4][:], in1=bb_f[:]),
                        wait=[("ve", t_m)], inc=["ve"])
                    wadd = [("ve", t_s2), ("dS", t_x)]
                    t_add2 = em.op("vector", lambda e, eo=eo, bi4=bi4: e.tensor_add(
                        out=x2_ev[bi4][:], in0=x2_ev[bi4][:],
                        in1=x_t[bi4][:]), wait=wadd, inc=["ve"])
                    xt_tok[bi4] = t_add2
                    x2ev_tok[bi4] = em.dma("sync", lambda e, eo=eo, sp=sp, bi4=bi4: e.dma_start(
                        out=x2T_sc[eo * 128:(eo + 1) * 128, sp * 512:(sp + 1) * 512],
                        in_=x2_ev[bi4][:]), "dS", wait=[("ve", t_add2)])
                aT_rel = t_add2
                abb_rel = t_add2
            emit_lnB_holder[0] = emit_lnB

            for sp in range(4):
                NK = 20 + 4 * sp
                d0 = 16 + 4 * sp
                wld0 = [("dS", t_consts)] if sp == 0 else []
                t_alb = em.dma("sync", lambda e, sp=sp: e.dma_start(
                    out=alb_s[:], in_=alb_d[:, sp * 512:(sp + 1) * 512]), "dS",
                    wait=wld0 + ([("sc", max(bank_rel))] if max(bank_rel) else []))
                t_msk = em.dma("sync", lambda e, sp=sp: e.dma_start(
                    out=maskb_s[:], in_=maskb_d[:, sp * 2048:(sp + 1) * 2048]), "dS",
                    wait=([("pe", probs_rel)] if probs_rel else []))
                t_UT_done = 0
                for h in range(16):
                    hb = h % 2
                    wld = [relh[hb]] if hb in relh else [("dS", t_msk)]
                    t_k = em.dma("sync", lambda e, h=h, hb=hb, NK=NK: e.dma_start(
                        out=kT_h[hb][:, 0:NK * 128],
                        in_=kT_sc[h * 128:(h + 1) * 128, 0:NK * 128]), "dS", wait=wld)
                    t_v = em.dma("sync", lambda e, h=h, hb=hb, NK=NK: e.dma_start(
                        out=v_h[hb][:, 0:NK * 128],
                        in_=v3_sc[h].rearrange("p a b -> p (a b)")[:, 0:NK * 128]), "dS")
                    t_q = em.dma("sync", lambda e, h=h, hb=hb, sp=sp: e.dma_start(
                        out=qT_h[hb][:], in_=qT_sc[h * 128:(h + 1) * 128, sp * 512:(sp + 1) * 512]),
                        "dS")
                    t_qb = em.dma("sync", lambda e, h=h, hb=hb, sp=sp: e.dma_start(
                        out=qb_b[hb][:], in_=qb_d[h:h + 1, sp * 512:(sp + 1) * 512]), "dS",
                        serialize=True)
                    use_qb = True
                    if use_qb:
                        wqb = [("dS", t_qb), ("ve", t_ones)]
                        if rb_rel:
                            wqb.append(rb_rel)
                        t_qm = em.op("tensor", lambda e, hb=hb: e.matmul(
                            out=p_rb[:], lhsT=ones_row[:], rhs=qb_b[hb][:],
                            start=True, stop=True), wait=wqb, inc=["pe"])
                        wqe = [("pe", t_qm)]
                        if probs_rel:
                            wqe.append(("pe", probs_rel))
                        t_qe = em.op("scalar", lambda e: e.copy(
                            out=qb_bcb[:], in_=p_rb[:]), wait=wqe, inc=["sc"])
                        rb_rel = ("sc", t_qe)
                    t_exp = 0
                    for kc in range(NK):
                        bi2 = kc % 4
                        bank = p_sc2[bi2]
                        w0 = [("dS", t_q)]
                        if bank_rel[bi2]:
                            w0.append(("sc", bank_rel[bi2]))
                        if kc == 0 and probs_rel:
                            w0.append(("pe", probs_rel))
                        diag = (d0 <= kc < d0 + 4)
                        started = False
                        if use_qb:
                            em.op("tensor", lambda e, bank=bank: e.matmul(
                                out=bank[:], lhsT=o128[:], rhs=qb_bcb[:],
                                start=True, stop=False),
                                wait=(w0 + ([("sc", t_qe)] if kc == 0 else [])))
                            started = True
                        if diag:
                            em.op("tensor", lambda e, kc=kc, d0=d0, bank=bank: e.matmul(
                                out=bank[:], lhsT=id_b[:],
                                rhs=maskb_s[:, (kc - d0) * 512:(kc - d0 + 1) * 512],
                                start=(not started), stop=False),
                                wait=(() if started else w0))
                            started = True
                        t_pe = em.op("tensor", lambda e, kc=kc, hb=hb, bank=bank, started=started: e.matmul(
                            out=bank[:], lhsT=kT_h[hb][:, kc * 128:(kc + 1) * 128],
                            rhs=qT_h[hb][:],
                            start=(not started), stop=True),
                            wait=(() if started else w0), inc=["pe"])
                        wex = [("pe", t_pe), ("dS", t_alb)]
                        t_exp = em.op("scalar", lambda e, kc=kc, sp=sp, h=h, bank=bank: e.activation(
                            out=probsT[:, kc * 512:(kc + 1) * 512], in_=bank[:], func=AF.Exp,
                            scale=INV_SQRT_D,
                            bias=alb_s[:, kc * 16 + h:kc * 16 + h + 1]),
                            wait=wex, inc=["sc"])
                        bank_rel[bi2] = t_exp
                    if dbg4 and (sp, h) == DBG_SPH:
                        em.dma("sync", lambda e: e.dma_start(out=dbg_t["p0"], in_=probsT[:]),
                               "dS", wait=[("sc", t_exp)])
                        em.dma("sync", lambda e: e.dma_start(out=dbg_t["qbc"], in_=qb_bcb[:]),
                               "dS", wait=[("sc", t_exp)])
                    emit_normB()
                    if h == 0 and emit_lnB_holder[0] is not None:
                        emit_lnB_holder[0]()
                    pvb = h % 2
                    t_pvs = 0
                    for kc in range(NK):
                        w1l = []
                        if kc == 0:
                            w1l = [("sc", t_exp)]
                            if pv_rel[pvb]:
                                w1l.append(("ve", pv_rel[pvb]))
                        em.op("tensor", lambda e, kc=kc, hb=hb, NK=NK, pvb=pvb: e.matmul(
                            out=p_pv[pvb][:], lhsT=v_h[hb][:, kc * 128:(kc + 1) * 128],
                            rhs=probsT[:, kc * 512:(kc + 1) * 512],
                            start=(kc == 0), stop=(kc == NK - 1)), wait=w1l)
                        w2l = []
                        if kc == 0 and sums_rel:
                            w2l = [sums_rel]
                        t_pvs = em.op("tensor", lambda e, kc=kc, NK=NK: e.matmul(
                            out=p_sums[:], lhsT=ones_col[:],
                            rhs=probsT[:, kc * 512:(kc + 1) * 512],
                            start=(kc == 0), stop=(kc == NK - 1)), wait=w2l,
                            inc=(["pe"] if kc == NK - 1 else []))
                    probs_rel = t_pvs
                    relh[hb] = ("pe", t_pvs)
                    t_r = em.op("vector", lambda e: e.reciprocal(out=a_row[:], in_=p_sums[:]),
                                wait=[("pe", t_pvs)], inc=["ve"])
                    sums_rel = ("ve", t_r)
                    t_hl = hilo_split(a_row, a_hi, a_lo, t_row, [("ve", t_r)])
                    pend = {"hl": t_hl, "h": h, "pvb": pvb}

                emit_normB()
                t_a_tok = [0] * 16
                for eo in range(16):
                    bi3 = eo % 2
                    bank = p_sc2[bi3]
                    w0 = [("dG", t_wo), ("ve", t_UT_done), ("sc", bank_rel[bi3])]
                    t_a = 0
                    for h16 in range(16):
                        t_a = em.op("tensor", lambda e, h16=h16, eo=eo, bank=bank: e.matmul(
                            out=bank[:], lhsT=wo_sb[:, (h16 * 16 + eo) * 128:(h16 * 16 + eo + 1) * 128],
                            rhs=UT_sp[:, h16 * 512:(h16 + 1) * 512],
                            start=(h16 == 0), stop=(h16 == 15)),
                            wait=(w0 if h16 == 0 else ()),
                            inc=(["pe"] if h16 == 15 else []))
                    wev = [("pe", t_a)]
                    if eo == 0 and aT_rel:
                        wev.append(("ve", aT_rel))
                    t_ev = em.op("scalar", lambda e, eo=eo, bank=bank: e.add(
                        out=aT_b[:, eo * 512:(eo + 1) * 512], in_=bank[:],
                        add=bo_s[:, eo:eo + 1]), wait=wev, inc=["sc"])
                    bank_rel[bi3] = t_ev
                    t_a_tok[eo] = (t_a, t_ev)
                UT_rel = t_a_tok[15][0]
                abc_rel = 0
                t_stats = 0
                for eo in range(16):
                    t_ev = t_a_tok[eo][1]
                    wc = [("sc", t_ev)]
                    if abc_rel:
                        wc.append(("pe", abc_rel))
                    t_s = em.op("scalar", lambda e, eo=eo: e.activation(
                        out=asq_c[:], in_=aT_b[:, eo * 512:(eo + 1) * 512], func=AF.Square),
                        wait=wc, inc=["sc"])
                    wm1 = [("sc", t_ev)]
                    if eo == 0:
                        wm1.append(sums_rel)
                        if stats_rel:
                            wm1.append(("sc", stats_rel))
                    em.op("tensor", lambda e, eo=eo: e.matmul(
                        out=p_sums[:], lhsT=ones_col[:], rhs=aT_b[:, eo * 512:(eo + 1) * 512],
                        start=(eo == 0), stop=(eo == 15)), wait=wm1)
                    wm2 = [("sc", t_s)]
                    if eo == 0:
                        wm2.append(rb_rel)
                    t_stats = em.op("tensor", lambda e, eo=eo: e.matmul(
                        out=p_rb[0:1, :], lhsT=ones_col[:], rhs=asq_c[:],
                        start=(eo == 0), stop=(eo == 15)), wait=wm2,
                        inc=["pe"])
                    abc_rel = t_stats
                t_evac = ln_evac(p_sums, p_rb[0:1, :], t_stats, 2048.0,
                                 extra_wait=([("ve", abb_rel)] if abb_rel else []))
                stats_rel = t_evac
                sums_rel = ("sc", t_evac)
                rb_rel = ("sc", t_evac)
                pend_ln = {"t_evac": t_evac, "sp": sp, "t_stats": t_stats}


            if emit_lnB_holder[0] is not None:
                emit_lnB_holder[0]()
            t_ph3_pe = CN["pe"]; t_ph3_sc = CN["sc"]; t_ph3_ve = CN["ve"]
            t_ph3_dS = CN["dS"]; t_ph3_dG = CN["dG"]

        if phases < 4:
            finish(True)
            return nc, dumps

        ph4 = ExitStack()
        with ph4:
            sb4 = lambda n, s, d: ph4.enter_context(nc.sbuf_tensor(n, list(s), d))
            xfT = sb4("xfT", (128, 16 * 1024), BF16)
            y_s = sb4("y_s", (128, 16 * 1024), F32)
            g_s = sb4("g_s", (128, NHC * 1024), BF16)
            cwT = sb4("cwT", (16, 1024), F32)
            c_hi16 = sb4("c_hi16", (16, 1024), BF16)
            c_lo16 = sb4("c_lo16", (16, 1024), BF16)
            c_hi_e = sb4("c_hi_e", (1, 1024), BF16)
            c_lo_e = sb4("c_lo_e", (1, 1024), BF16)
            cb_f = [sb4(f"cb_f{i}", (128, 512), F32) for i in range(2)]
            w1_s = [sb4(f"w1_s{i}", (128, 2048), BF16) for i in range(2)]
            w3_s = [sb4(f"w3_s{i}", (128, 2048), BF16) for i in range(2)]
            w2_s = [sb4(f"w2_s{i}", (128, NHC * 128), BF16) for i in range(4)]
            b2row_b = sb4("b2row_b", (1, 2048), BF16)
            silu_f = sb4("silu_f", (128, 512), F32)
            tg_f = sb4("tg_f", (128, 512), F32)
            scores_sb = sb4("scores_sb", (128, 128), F32)
            gmx = sb4("gmx", (128, 1), F32)
            ge_f = sb4("ge_f", (128, 16), F32)
            gsum = sb4("gsum", (128, 1), F32)
            gpr = sb4("gpr", (128, 16), F32)
            gbi = sb4("gbi", (128, 16), F32)
            gm8 = sb4("gm8", (128, 8), F32)
            gmask = sb4("gmask", (128, 16), F32)
            gcw = sb4("gcw", (128, 16), F32)
            out_ev = [sb4(f"out_ev{i}", (128, 512), F32) for i in range(2)]
            x2_t = [sb4(f"x2_t{i}", (128, 512), F32) for i in range(2)]
            y_addr = nc.lookup_mloc(y_s).addr
            x2_f = nc.alloc_sbuf_tensor_at("x2_f_alias", [128, 8192], F32, offset=y_addr)
            x2b = nc.alloc_sbuf_tensor_at("x2b_alias", [128, 8192], BF16, offset=y_addr + 32768)
            x2sq = nc.alloc_sbuf_tensor_at("x2sq_alias", [128, 8192], BF16, offset=y_addr + 49152)

            join()
            bnd4 = []
            first4 = [True]
            out_tok = [0, 0]
            x2t_tok = [0, 0]

            for hf in range(2):
                ph4a = ExitStack()
                with ph4a:
                    ps4a = lambda n, s: ph4a.enter_context(nc.psum_tensor(f"{n}_h{hf}", list(s), F32))
                    p_s14 = ps4a("p_s14", (1, 512))
                    p_s24 = ps4a("p_s24", (1, 512))
                    p_ab4 = ps4a("p_ab4", (128, 512))
                    p_bb4 = ps4a("p_bb4", (128, 512))
                    p_g4 = ps4a("p_g4", (128, 16))
                    rel4 = {}
                    pg_rel = 0
                    t_sgate = 0
                    for sp2 in range(2):
                        col0 = hf * 1024 + sp2 * 512
                        tx = 0
                        for ec in range(16):
                            w0 = list(bnd4) if first4[0] else (
                                [rel4["x2_f"]] if ("x2_f" in rel4 and ec == 0) else [])
                            first4[0] = False
                            tx = em.dma("sync", lambda e, ec=ec, col0=col0: e.dma_start(
                                out=x2_f[:, ec * 512:(ec + 1) * 512],
                                in_=x2T_sc[ec * 128:(ec + 1) * 128, col0:col0 + 512]),
                                "dS", wait=w0)
                        wc = [("dS", tx)]
                        if "x2b" in rel4:
                            wc.append(rel4["x2b"])
                        t_cast = em.op("vector", lambda e: e.tensor_copy(out=x2b[:], in_=x2_f[:]),
                                       wait=wc, inc=["ve"])
                        wsq = [("dS", tx)] + ([rel4["x2sq"]] if "x2sq" in rel4 else [])
                        t_sq = em.op("scalar", lambda e: e.activation(out=x2sq[:], in_=x2_f[:], func=AF.Square),
                                     wait=wsq, inc=["sc"])
                        wst = [("ve", t_cast)]
                        if "p_st4" in rel4:
                            wst.append(rel4["p_st4"])
                        for ec in range(16):
                            em.op("tensor", lambda e, ec=ec: e.matmul(
                                out=p_s14[:], lhsT=ones_col[:], rhs=x2b[:, ec * 512:(ec + 1) * 512],
                                start=(ec == 0), stop=(ec == 15)), wait=(wst if ec == 0 else ()))
                        t_stats = 0
                        for ec in range(16):
                            t_stats = em.op("tensor", lambda e, ec=ec: e.matmul(
                                out=p_s24[:], lhsT=ones_col[:], rhs=x2sq[:, ec * 512:(ec + 1) * 512],
                                start=(ec == 0), stop=(ec == 15)),
                                wait=([("sc", t_sq)] if ec == 0 else ()),
                                inc=(["pe"] if ec == 15 else []))
                        rel4["x2b"] = ("pe", t_stats)
                        rel4["x2sq"] = ("pe", t_stats)
                        ew = [rel4["abb4"]] if "abb4" in rel4 else []
                        t_ab = ln_rows_and_bcast(p_s14, p_s24, p_ab4, p_bb4, t_stats, 2048.0,
                                                 extra_wait=ew)
                        rel4["p_st4"] = ("sc", t_ab)
                        tv = 0
                        tsub = 0
                        for ec in range(16):
                            tm = em.op("vector", lambda e, ec=ec: e.tensor_mul(
                                out=x2_f[:, ec * 512:(ec + 1) * 512],
                                in0=x2_f[:, ec * 512:(ec + 1) * 512], in1=ab_f[:]),
                                wait=([("sc", t_ab), ("pe", t_stats)] if ec == 0 else ()), inc=["ve"])
                            tsub = em.op("vector", lambda e, ec=ec: e.tensor_sub(
                                out=x2_f[:, ec * 512:(ec + 1) * 512],
                                in0=x2_f[:, ec * 512:(ec + 1) * 512], in1=bb_f[:]),
                                wait=[("ve", tm)], inc=["ve"])
                            tv = em.op("vector", lambda e, ec=ec, sp2=sp2: e.tensor_copy(
                                out=xfT[:, ec * 1024 + sp2 * 512: ec * 1024 + (sp2 + 1) * 512],
                                in_=x2_f[:, ec * 512:(ec + 1) * 512]),
                                wait=[("ve", tsub)], inc=["ve"])
                        rel4["abb4"] = ("ve", tv)
                        for tc4 in range(4):
                            tgm = 0
                            for ec in range(16):
                                wgm = []
                                if ec == 0:
                                    wgm = [("ve", tv), ("dS", t_gw)]
                                    if pg_rel:
                                        wgm.append(("sc", pg_rel))
                                tgm = em.op("tensor", lambda e, ec=ec, tc4=tc4: e.matmul(
                                    out=p_g4[:], lhsT=x2_f[:, ec * 512 + tc4 * 128: ec * 512 + (tc4 + 1) * 128],
                                    rhs=gw_s[:, ec * 16:(ec + 1) * 16],
                                    start=(ec == 0), stop=(ec == 15)),
                                    wait=wgm, inc=(["pe"] if ec == 15 else []))
                            t_sgate = em.op("scalar", lambda e, sp2=sp2, tc4=tc4: e.copy(
                                out=scores_sb[:, (sp2 * 4 + tc4) * 16:(sp2 * 4 + tc4 + 1) * 16],
                                in_=p_g4[:]), wait=[("pe", tgm)], inc=["sc"])
                            pg_rel = t_sgate
                        rel4["x2_f"] = ("pe", tgm)
                    t_xf = tv
                    t_scores = t_sgate
                if dbg4 and hf == 0:
                    em.dma("sync", lambda e: e.dma_start(out=dbg_t["sc0"], in_=scores_sb[:]),
                           "dS", wait=[("sc", t_scores)])
                    em.dma("sync", lambda e: e.dma_start(out=dbg_t["xfT"], in_=xfT[:]),
                           "dS", wait=[("ve", t_xf)])

                ph4b = ExitStack()
                with ph4b:
                    ps4b = lambda n, s: ph4b.enter_context(nc.psum_tensor(f"{n}_h{hf}", list(s), F32))
                    p_cw = ps4b("p_cw", (16, 128))
                    g_prev_ve = 0
                    pcw_rel = 0
                    for tc in range(8):
                        w1g = [("sc", t_scores)]
                        if g_prev_ve:
                            w1g.append(("ve", g_prev_ve))
                        t1 = em.op("vector", lambda e, tc=tc: e.reduce_max(
                            out=gmx[:], in_=scores_sb[:, tc * 16:(tc + 1) * 16], axis=AX, negate=True),
                                   wait=w1g, inc=["ve"])
                        t2 = em.op("scalar", lambda e, tc=tc: e.activation(
                            out=ge_f[:], in_=scores_sb[:, tc * 16:(tc + 1) * 16], func=AF.Exp,
                            bias=gmx[:, 0:1]),
                                   wait=[("ve", t1)], inc=["sc"])
                        t3 = em.op("vector", lambda e: e.reduce_sum(out=gsum[:], in_=ge_f[:], axis=AX),
                                   wait=[("sc", t2)], inc=["ve"])
                        t4 = em.op("vector", lambda e: e.reciprocal(out=gsum[:], in_=gsum[:]),
                                   wait=[("ve", t3)], inc=["ve"])
                        t5 = em.op("vector", lambda e: e.tensor_scalar_mul(
                            out=gpr[:], in0=ge_f[:], scalar1=gsum[:, 0:1]),
                            wait=[("ve", t4)], inc=["ve"])
                        t6 = em.op("vector", lambda e: e.tensor_add(out=gbi[:], in0=gpr[:], in1=gb_s[:]),
                                   wait=[("ve", t5)], inc=["ve"])
                        t7 = em.op("vector", lambda e: e.max(out=gm8[:], in_=gbi[:]),
                                   wait=[("ve", t6)], inc=["ve"])
                        t8 = em.op("vector", lambda e: e.tensor_scalar(
                            out=gmask[:], in0=gbi[:], scalar1=gm8[:, 5:6], scalar2=None, op0=OP.is_ge),
                            wait=[("ve", t7)], inc=["ve"])
                        t9 = em.op("vector", lambda e: e.tensor_mul(out=gcw[:], in0=gpr[:], in1=gmask[:]),
                                   wait=[("ve", t8)], inc=["ve"])
                        g_prev_ve = t9
                        wtr = [("ve", t9), ("dS", t_consts)]
                        if pcw_rel:
                            wtr.append(("sc", pcw_rel))
                        t_tr = em.op("tensor", lambda e: e.transpose(out=p_cw[:], in_=gcw[:], identity=id_f[:]),
                                     wait=wtr, inc=["pe"])
                        pcw_rel = em.op("scalar", lambda e, tc=tc: e.copy(
                            out=cwT[:, tc * 128:(tc + 1) * 128], in_=p_cw[:]),
                            wait=[("pe", t_tr)], inc=["sc"])
                    wchl = [("sc", pcw_rel)]
                    if dbg4 and hf == 0:
                        t_dcw = em.dma("sync", lambda e: e.dma_start(out=dbg_t["cwT"], in_=cwT[:]),
                                       "dS", wait=[("sc", pcw_rel)])
                        wchl.append(("dS", t_dcw))
                    t_chl = hilo_split(cwT, c_hi16, c_lo16, cwT, wchl)

                ph4c = ExitStack()
                with ph4c:
                    ps4c = lambda n, s: ph4c.enter_context(nc.psum_tensor(f"{n}_h{hf}", list(s), F32))
                    p_h1 = [ps4c(f"p_h1_{i}", (128, 512)) for i in range(2)]
                    p_h3 = [ps4c(f"p_h3_{i}", (128, 512)) for i in range(2)]
                    p_y = [ps4c(f"p_y{i}", (128, 512)) for i in range(4)]

                    h1_rel = [0, 0]; h3_rel = [0, 0]; y_rel = [0, 0, 0, 0]
                    cb_rel = [0, 0]; pcb_rel = 0
                    w1_rel = [0, 0]; w3_rel = [0, 0]; w2_rel = [0, 0, 0, 0]
                    b2_tok = 0
                    che_tok = 0
                    g_rel = 0
                    t_cbf = 0

                    for e_i in range(18):
                        is_sh = (e_i >= 16)
                        grp = e_i - 16
                        hc0 = grp * NHC if is_sh else 0
                        if not is_sh or grp == 0:
                            wb2 = [("pe", che_tok)] if che_tok else []
                            b2_tok = em.dma("gpsimd", lambda e, e_i=e_i, is_sh=is_sh: e.dma_start(
                                out=b2row_b[:], in_=(bs2H if is_sh else b2H[e_i:e_i + 1, :])), "dG",
                                wait=wb2, serialize=True)
                        if not is_sh:
                            wch = [("ve", t_chl)]
                            if che_tok:
                                wch.append(("pe", che_tok))
                            t_ch = em.dma("sync", lambda e, e_i=e_i: e.dma_start(
                                out=c_hi_e[:], in_=c_hi16[e_i:e_i + 1, :]), "dS", wait=wch,
                                serialize=True)
                            t_cl = em.dma("sync", lambda e, e_i=e_i: e.dma_start(
                                out=c_lo_e[:], in_=c_lo16[e_i:e_i + 1, :]), "dS")
                            for sp2 in range(2):
                                wcb = [("dS", t_cl)]
                                if pcb_rel:
                                    wcb.append(("sc", pcb_rel))
                                if h1_rel[0]:
                                    wcb += [("sc", h1_rel[0]), ("sc", h1_rel[1])]
                                em.op("tensor", lambda e, sp2=sp2: e.matmul(
                                    out=p_h1[0][:], lhsT=ones_row[:],
                                    rhs=c_hi_e[0:1, sp2 * 512:(sp2 + 1) * 512],
                                    start=True, stop=False), wait=wcb)
                                t_cbm = em.op("tensor", lambda e, sp2=sp2: e.matmul(
                                    out=p_h1[0][:], lhsT=ones_row[:],
                                    rhs=c_lo_e[0:1, sp2 * 512:(sp2 + 1) * 512],
                                    start=False, stop=True), inc=["pe"])
                                wcf = [("pe", t_cbm)]
                                if cb_rel[sp2]:
                                    wcf.append(("ve", cb_rel[sp2]))
                                pcb_rel = em.op("scalar", lambda e, sp2=sp2: e.copy(
                                    out=cb_f[sp2][:], in_=p_h1[0][:, 0:512]), wait=wcf, inc=["sc"])
                                h1_rel[0] = pcb_rel
                            t_cbf = pcb_rel
                        t_gv = 0
                        for hci in range(NHC):
                            hc = hc0 + hci
                            wi = hci % 2
                            W1s = ws1H if is_sh else w1H[e_i]
                            W3s = ws3H if is_sh else w3H[e_i]
                            ww = [("pe", w1_rel[wi])] if w1_rel[wi] else []
                            w1_tok = em.dma("gpsimd", lambda e, hc=hc, wi=wi, W1s=W1s: e.dma_start(
                                out=w1_s[wi][:], in_=W1s[hc].rearrange("p a b -> p (a b)")), "dG",
                                wait=ww)
                            ww3 = [("pe", w3_rel[wi])] if w3_rel[wi] else []
                            w3_tok = em.dma("gpsimd", lambda e, hc=hc, wi=wi, W3s=W3s: e.dma_start(
                                out=w3_s[wi][:], in_=W3s[hc].rearrange("p a b -> p (a b)")), "dG",
                                wait=ww3)
                            wh1 = [("dG", w1_tok), ("ve", t_xf)]
                            if h1_rel[0]:
                                wh1 += [("sc", h1_rel[0]), ("sc", h1_rel[1])]
                            t_h1m = 0
                            for ec in range(16):
                                for sp2 in range(2):
                                    t_h1m = em.op("tensor", lambda e, ec=ec, sp2=sp2, wi=wi: e.matmul(
                                        out=p_h1[sp2][:], lhsT=w1_s[wi][:, ec * 128:(ec + 1) * 128],
                                        rhs=xfT[:, ec * 1024 + sp2 * 512: ec * 1024 + (sp2 + 1) * 512],
                                        start=(ec == 0), stop=(ec == 15)),
                                        wait=(wh1 if (ec == 0 and sp2 == 0) else ()),
                                        inc=(["pe"] if (ec == 15 and sp2 == 1) else []))
                            w1_rel[wi] = t_h1m
                            wh3 = [("dG", w3_tok)]
                            if h3_rel[0]:
                                wh3 += [("ve", h3_rel[0]), ("ve", h3_rel[1])]
                            t_h3m = 0
                            for ec in range(16):
                                for sp2 in range(2):
                                    t_h3m = em.op("tensor", lambda e, ec=ec, sp2=sp2, wi=wi: e.matmul(
                                        out=p_h3[sp2][:], lhsT=w3_s[wi][:, ec * 128:(ec + 1) * 128],
                                        rhs=xfT[:, ec * 1024 + sp2 * 512: ec * 1024 + (sp2 + 1) * 512],
                                        start=(ec == 0), stop=(ec == 15)),
                                        wait=(wh3 if (ec == 0 and sp2 == 0) else ()),
                                        inc=(["pe"] if (ec == 15 and sp2 == 1) else []))
                            w3_rel[wi] = t_h3m
                            for sp2 in range(2):
                                bcol = (bs1_s[:, hc:hc + 1] if is_sh
                                        else b1_s[:, e_i * NHC + hc: e_i * NHC + hc + 1])
                                b3col = (bs3_s[:, hc:hc + 1] if is_sh
                                         else b3_s[:, e_i * NHC + hc: e_i * NHC + hc + 1])
                                t_si = em.op("scalar", lambda e, sp2=sp2, bcol=bcol: e.activation(
                                    out=silu_f[:], in_=p_h1[sp2][:], func=AF.Silu, bias=bcol),
                                    wait=[("pe", t_h1m)] + ([("ve", t_gv)] if t_gv else []),
                                    inc=["sc"])
                                h1_rel[sp2] = t_si
                                wg1 = [("sc", t_si), ("pe", t_h3m)]
                                t_g1 = em.op("vector", lambda e, sp2=sp2, b3col=b3col: e.scalar_tensor_tensor(
                                    out=tg_f[:], in0=p_h3[sp2][:], scalar=b3col, in1=silu_f[:],
                                    op0=OP.add, op1=OP.mult), wait=wg1, inc=["ve"])
                                h3_rel[sp2] = t_g1
                                wg2 = [("ve", t_g1)]
                                if hci == 0 and sp2 == 0 and g_rel:
                                    wg2.append(("pe", g_rel))
                                if is_sh:
                                    t_gv = em.op("vector", lambda e, hci=hci, sp2=sp2: e.tensor_copy(
                                        out=g_s[:, hci * 1024 + sp2 * 512: hci * 1024 + (sp2 + 1) * 512],
                                        in_=tg_f[:]), wait=wg2, inc=["ve"])
                                else:
                                    t_gv = em.op("vector", lambda e, hci=hci, sp2=sp2: e.tensor_mul(
                                        out=g_s[:, hci * 1024 + sp2 * 512: hci * 1024 + (sp2 + 1) * 512],
                                        in0=tg_f[:], in1=cb_f[sp2][:]),
                                        wait=wg2 + ([("sc", t_cbf)] if hci == 0 else []), inc=["ve"])
                        if dbg4 and hf == 0 and e_i == 0:
                            em.dma("sync", lambda e: e.dma_start(out=dbg_t["g0"], in_=g_s[:]),
                                   "dS", wait=[("ve", t_gv)])
                            em.dma("sync", lambda e: e.dma_start(out=dbg_t["cb0"][:, 0:512], in_=cb_f[0][:]),
                                   "dS", wait=[("ve", t_gv)])
                            em.dma("sync", lambda e: e.dma_start(out=dbg_t["cb0"][:, 512:1024], in_=cb_f[1][:]),
                                   "dS", wait=[("ve", t_gv)])
                        if not is_sh:
                            cb_rel[0] = cb_rel[1] = t_gv
                        W2s = ws2H if is_sh else w2H[e_i]
                        t_ym = 0
                        for dc in range(16):
                            wi = dc % 4
                            ww2 = [("pe", w2_rel[wi])] if w2_rel[wi] else []
                            w2_tok = em.dma("gpsimd", lambda e, dc=dc, wi=wi, W2s=W2s, hc0=hc0: e.dma_start(
                                out=w2_s[wi][:],
                                in_=W2s[dc, :, hc0:hc0 + NHC, :].rearrange("p a b -> p (a b)")), "dG", wait=ww2)
                            for sp2 in range(2):
                                yb_i = (dc % 2) * 2 + sp2
                                wy0 = [("dG", w2_tok), ("ve", t_gv)]
                                if y_rel[yb_i]:
                                    wy0.append(("ve", y_rel[yb_i]))
                                if is_sh and grp == 1:
                                    pass
                                else:
                                    wy0b = wy0 + ([("dG", b2_tok)] if b2_tok else [])
                                    rhs_c = (ones_q if is_sh else c_hi_e)
                                    em.op("tensor", lambda e, dc=dc, sp2=sp2, rhs_c=rhs_c, is_sh=is_sh, yb_i=yb_i: e.matmul(
                                        out=p_y[yb_i][:],
                                        lhsT=b2row_b[0:1, dc * 128:(dc + 1) * 128],
                                        rhs=(rhs_c[0:1, 0:512] if is_sh else rhs_c[0:1, sp2 * 512:(sp2 + 1) * 512]),
                                        start=True, stop=False), wait=wy0b)
                                for hci in range(NHC):
                                    t_ym = em.op("tensor", lambda e, hci=hci, sp2=sp2, wi=wi, is_sh=is_sh, grp=grp, yb_i=yb_i: e.matmul(
                                        out=p_y[yb_i][:], lhsT=w2_s[wi][:, hci * 128:(hci + 1) * 128],
                                        rhs=g_s[:, hci * 1024 + sp2 * 512: hci * 1024 + (sp2 + 1) * 512],
                                        start=(is_sh and grp == 1 and hci == 0),
                                        stop=(hci == NHC - 1)),
                                        wait=(wy0 if (is_sh and grp == 1 and hci == 0) else ()),
                                        inc=(["pe"] if hci == NHC - 1 else []))
                                w2_rel[wi] = t_ym
                                wya = [("pe", t_ym)]
                                if e_i == 0:
                                    t_ya = em.op("vector", lambda e, dc=dc, sp2=sp2, yb_i=yb_i: e.tensor_copy(
                                        out=y_s[:, dc * 1024 + sp2 * 512: dc * 1024 + (sp2 + 1) * 512],
                                        in_=p_y[yb_i][:]), wait=wya, inc=["ve"])
                                else:
                                    t_ya = em.op("vector", lambda e, dc=dc, sp2=sp2, yb_i=yb_i: e.tensor_add(
                                        out=y_s[:, dc * 1024 + sp2 * 512: dc * 1024 + (sp2 + 1) * 512],
                                        in0=y_s[:, dc * 1024 + sp2 * 512: dc * 1024 + (sp2 + 1) * 512],
                                        in1=p_y[yb_i][:]), wait=wya, inc=["ve"])
                                y_rel[yb_i] = t_ya
                        g_rel = t_ym
                        che_tok = t_ym
                        if dbg4 and hf == 0 and e_i == 0:
                            em.dma("sync", lambda e: e.dma_start(out=dbg_t["y0"], in_=y_s[:]),
                                   "dS", wait=[("ve", y_rel[1])])
                    t_y_done = max(y_rel)
                if dbg4 and hf == 0:
                    em.dma("sync", lambda e: e.dma_start(out=dbg_t["y"], in_=y_s[:]),
                           "dS", wait=[("ve", t_y_done)])

                ph4d = ExitStack()
                with ph4d:
                    sb4d = lambda n, s, d: ph4d.enter_context(nc.sbuf_tensor(f"{n}_h{hf}", list(s), d))
                    ps4d = lambda n, s: ph4d.enter_context(nc.psum_tensor(f"{n}_h{hf}", list(s), F32))
                    yb_c = sb4d("yb_c", (128, 512), BF16)
                    ysq_c = sb4d("ysq_c", (128, 512), BF16)
                    p_s1d = ps4d("p_s1d", (1, 512))
                    p_s2d = ps4d("p_s2d", (1, 512))
                    p_abd = ps4d("p_abd", (128, 512))
                    p_bbd = ps4d("p_bbd", (128, 512))
                    ybc_rel = 0
                    statsd_rel = 0
                    abbd_rel = 0
                    for sp2 in range(2):
                        t_stats = 0
                        for dc in range(16):
                            col = dc * 1024 + sp2 * 512
                            wc = [("ve", t_y_done)]
                            if ybc_rel:
                                wc.append(("pe", ybc_rel))
                            t_c = em.op("vector", lambda e, col=col: e.tensor_copy(
                                out=yb_c[:], in_=y_s[:, col:col + 512]), wait=wc, inc=["ve"])
                            t_s = em.op("scalar", lambda e, col=col: e.activation(
                                out=ysq_c[:], in_=y_s[:, col:col + 512], func=AF.Square),
                                wait=wc, inc=["sc"])
                            wm1 = [("ve", t_c)]
                            if dc == 0 and statsd_rel:
                                wm1.append(("sc", statsd_rel))
                            em.op("tensor", lambda e, dc=dc: e.matmul(
                                out=p_s1d[:], lhsT=ones_col[:], rhs=yb_c[:],
                                start=(dc == 0), stop=(dc == 15)), wait=wm1)
                            t_stats = em.op("tensor", lambda e, dc=dc: e.matmul(
                                out=p_s2d[:], lhsT=ones_col[:], rhs=ysq_c[:],
                                start=(dc == 0), stop=(dc == 15)), wait=[("sc", t_s)],
                                inc=["pe"])
                            ybc_rel = t_stats
                        ew = [("ve", abbd_rel)] if abbd_rel else []
                        t_ab = ln_rows_and_bcast(p_s1d, p_s2d, p_abd, p_bbd, t_stats, 2048.0,
                                                 extra_wait=ew)
                        statsd_rel = t_ab
                        t_last = 0
                        for dc in range(16):
                            col = dc * 1024 + sp2 * 512
                            bi4 = dc % 2
                            wx = [("ve", x2t_tok[bi4])] if x2t_tok[bi4] else []
                            t_x2 = em.dma("sync", lambda e, dc=dc, sp2=sp2, hf=hf, bi4=bi4: e.dma_start(
                                out=x2_t[bi4][:], in_=x2T_sc[dc * 128:(dc + 1) * 128,
                                                           hf * 1024 + sp2 * 512: hf * 1024 + (sp2 + 1) * 512]),
                                "dS", wait=wx)
                            t_m = em.op("vector", lambda e, col=col: e.tensor_mul(
                                out=y_s[:, col:col + 512], in0=y_s[:, col:col + 512], in1=ab_f[:]),
                                wait=[("sc", t_ab), ("pe", t_stats)], inc=["ve"])
                            t_s2 = em.op("vector", lambda e, col=col: e.tensor_sub(
                                out=y_s[:, col:col + 512], in0=y_s[:, col:col + 512], in1=bb_f[:]),
                                wait=[("ve", t_m)], inc=["ve"])
                            wadd = [("ve", t_s2), ("dS", t_x2)]
                            if out_tok[bi4]:
                                wadd.append(("dS", out_tok[bi4]))
                            t_add = em.op("vector", lambda e, col=col, bi4=bi4: e.tensor_add(
                                out=out_ev[bi4][:], in0=y_s[:, col:col + 512], in1=x2_t[bi4][:]),
                                wait=wadd, inc=["ve"])
                            x2t_tok[bi4] = t_add
                            out_tok[bi4] = em.dma("sync", lambda e, dc=dc, sp2=sp2, hf=hf, bi4=bi4: e.dma_start(
                                out=outT[dc * 128:(dc + 1) * 128,
                                         hf * 1024 + sp2 * 512: hf * 1024 + (sp2 + 1) * 512],
                                in_=out_ev[bi4][:]), "dS", wait=[("ve", t_add)])
                            t_last = t_add
                        abbd_rel = t_last
                join()
        finish(False)
        return nc, dumps

    return nc, dumps



def kernel(**inputs):
    sh, per_core = host_prep(inputs)
    nc, _ = build_program(phases=4)
    in_maps = []
    for c in range(8):
        m = dict(sh)
        m.update(per_core[c])
        in_maps.append(m)
    res = run_bass_kernel_spmd(nc, in_maps, core_ids=list(range(8)))
    out = np.empty((4, 4096, 2048), dtype=np.float32)
    for c in range(8):
        b, half = c // 2, c % 2
        out[b, half * 2048:(half + 1) * 2048, :] = np.asarray(res.results[c]["outT"]).T
    return out
